# version 01 (final)
# baseline (speedup 1.0000x reference)
import numpy as np
import ml_dtypes
from contextlib import ExitStack
import concourse.bass as bass
import concourse.mybir as mybir
from concourse.bass_utils import run_bass_kernel_spmd

F32 = mybir.dt.float32
BF16 = mybir.dt.bfloat16
I32 = mybir.dt.int32
AF = mybir.ActivationFunctionType
ALU = mybir.AluOpType
AX = mybir.AxisListType
BFNP = ml_dtypes.bfloat16

D = 1024
NCORE = 8
LN_EPS = 1e-5


class Sem:

    def __init__(self, h):
        self.h = h
        self.n = 0

    def inc(self, instr, k=1):
        instr.then_inc(self.h, k)
        self.n += k
        return self.n


def emit_ln_chunk(block, nc, x_ap_fn, out_ap_fn, stats, s_v, s_sc, nblk=4,
                  pre_wait=None):
    @block.vector
    def _(e):
        if pre_wait is not None:
            pre_wait(e)
        for b in range(nblk):
            x_ap = x_ap_fn(b)
            e.bn_stats(stats[:, 12 * b:12 * b + 6], x_ap[:, 0:512])
            ins = e.bn_stats(stats[:, 12 * b + 6:12 * b + 12], x_ap[:, 512:1024])
        s_v.inc(ins, 1)
        e.wait_ge(s_v.h, s_v.n)
        for b in range(nblk):
            ins = e.bn_aggr(stats[:, 48 + 2 * b:50 + 2 * b], stats[:, 12 * b:12 * b + 12])
        s_v.inc(ins, 1)
        e.wait_ge(s_v.h, s_v.n)
        for b in range(nblk):
            ins = e.tensor_scalar_add(stats[:, 56 + b:57 + b],
                                      stats[:, 49 + 2 * b:50 + 2 * b], LN_EPS)
        s_v.inc(ins, 1)

    @block.scalar
    def _(e):
        e.wait_ge(s_v.h, s_v.n)
        s_sc.inc(e.sqrt(stats[:, 60:60 + nblk], stats[:, 56:56 + nblk]), 1)

    @block.vector
    def _(e):
        e.wait_ge(s_sc.h, s_sc.n)
        s_v.inc(e.reciprocal(stats[:, 56:56 + nblk], stats[:, 60:60 + nblk]), 1)
        e.wait_ge(s_v.h, s_v.n)
        for b in range(nblk):
            ins = e.tensor_scalar(out_ap_fn(b), x_ap_fn(b),
                                  stats[:, 48 + 2 * b:49 + 2 * b],
                                  stats[:, 56 + b:57 + b], ALU.subtract, ALU.mult)
        s_v.inc(ins, 1)


def emit_transpose_chunk(block, nc, src_fn, dst_fn, eye_sb, ps_t, s_t, s_sc,
                         ngroups, copy_engine="scalar"):
    copy_acks = []
    for g in range(ngroups):
        bank = g % 2

        @block.tensor
        def _(e, g=g, bank=bank):
            if g >= 2:
                e.wait_ge(s_sc.h, copy_acks[g - 2])
            for j in range(4):
                ins = e.transpose(ps_t[bank][:, j * 128:(j + 1) * 128],
                                  src_fn(g, j), eye_sb[:, :])
            s_t.inc(ins, 1)

        eng = getattr(block, copy_engine)

        @eng
        def _(e, g=g, bank=bank):
            e.wait_ge(s_t.h, s_t.n)
            if copy_engine == "scalar":
                ins = e.copy(dst_fn(g), ps_t[bank][:, :])
            else:
                ins = e.tensor_copy(dst_fn(g), ps_t[bank][:, :])
            copy_acks.append(s_sc.inc(ins, 1))


def build_mlp(T=2048, F=4096):
    nc = bass.Bass("TRN2", target_bir_lowering=False)
    NB, NCH, NF = T // 128, T // 512, F // 128
    x_in = nc.dram_tensor("x", [T, D], F32, kind="ExternalInput").ap()
    w1_in = nc.dram_tensor("w1", [D, F], F32, kind="ExternalInput").ap()
    b1_in = nc.dram_tensor("b1c", [128, NF], F32, kind="ExternalInput").ap()
    w2_in = nc.dram_tensor("w2", [F, D], F32, kind="ExternalInput").ap()
    b2_in = nc.dram_tensor("b2r", [1, D], F32, kind="ExternalInput").ap()
    eye_in = nc.dram_tensor("eye", [128, 128], BF16, kind="ExternalInput").ap()
    y_out = nc.dram_tensor("y", [T, D], F32, kind="ExternalOutput").ap()

    x_src = x_in.rearrange("(t p) d -> p t d", p=128)
    y_dst = y_out.rearrange("(t p) d -> p t d", p=128)
    w1_src = w1_in.rearrange("(k p) f -> p k f", p=128)
    w2_src = w2_in.rearrange("(k p) d -> p k d", p=128)

    with ExitStack() as ctx:
        sb = lambda name, shape, dt: ctx.enter_context(nc.sbuf_tensor(name, shape, dt))
        ps = lambda name, shape, dt: ctx.enter_context(nc.psum_tensor(name, shape, dt))
        sem = lambda name: Sem(ctx.enter_context(nc.semaphore(name)))
        x_sb = sb("x_sb", [128, NB * D], F32)
        xn_bf = sb("xn_bf", [128, 4 * D], BF16)
        xn_fm = sb("xn_fm", [128, 8 * T], BF16)
        h_bf = sb("h_bf", [128, NF * 512], BF16)
        w1_buf = sb("w1_buf", [128, 2 * 8 * 128], BF16)
        w2_bf = sb("w2_bf", [128, NF * D], BF16)
        b1_sb = sb("b1_sb", [128, NF], F32)
        b2_sb = sb("b2_sb", [1, D], BF16)
        ones1 = sb("ones1", [1, 128], BF16)
        eye_sb = sb("eye_sb", [128, 128], BF16)
        stats = sb("stats", [128, 64], F32)
        ps_h0 = ps("ps_h0", [128, 512], F32)
        ps_h1 = ps("ps_h1", [128, 512], F32)
        ps_y0 = ps("ps_y0", [128, 512], F32)
        ps_y1 = ps("ps_y1", [128, 512], F32)
        ps_t0 = ps("ps_t0", [128, 512], BF16)
        ps_t1 = ps("ps_t1", [128, 512], BF16)
        s_x, s_w, s_v = sem("s_x"), sem("s_w"), sem("s_v")
        s_sc, s_t, s_st = sem("s_sc"), sem("s_t"), sem("s_st")
        s_wa, s_wb = sem("s_wa"), sem("s_wb")
        s_w1 = [s_wa, s_wb]
        block = ctx.enter_context(nc.Block())
        ps_h, ps_y, ps_t = [ps_h0, ps_h1], [ps_y0, ps_y1], [ps_t0, ps_t1]
        x3 = x_sb[:].rearrange("p (t d) -> p t d", d=D)

        @block.sync
        def _(e):
            for c in range(NCH):
                s_x.inc(e.dma_start(out=x3[:, 4 * c:4 * (c + 1), :],
                                    in_=x_src[:, 4 * c:4 * (c + 1), :]), 16)
            s_x.inc(e.dma_start(out=b1_sb[:, :], in_=b1_in[:, :]), 16)
            s_x.inc(e.dma_start(out=eye_sb[:, :], in_=eye_in[:, :]), 16)

        @block.gpsimd
        def _(e):
            s_w.inc(e.dma_start(out=b2_sb[:, :], in_=b2_in[:, :]), 16)
            w2v = w2_bf[:].rearrange("p (k d) -> p k d", d=D)
            q = NF // 4
            for i in range(4):
                s_w.inc(e.dma_start(out=w2v[:, q * i:q * (i + 1), :],
                                    in_=w2_src[:, q * i:q * (i + 1), :]), 16)

        @block.vector
        def _(e):
            s_v.inc(e.memset(ones1[:, :], 1.0), 1)

        w1v = w1_buf[:].rearrange("p (b k f) -> p b k f", b=2, k=8)
        h_group_acks = []
        h_act_acks = []
        y_evac_acks = []
        w1_sent = []

        @block.gpsimd
        def _(e):
            for i in range(2):
                w1_sent.append(
                    s_w1[i % 2].inc(e.dma_start(out=w1v[:, i % 2, :, :],
                                                in_=w1_src[:, :, i * 128:(i + 1) * 128]), 16))

        eye_wait_done = [False]

        for c in range(NCH):
            def pre(e, c=c):
                e.wait_ge(s_x.h, s_x.n)
                if c > 0:
                    e.wait_ge(s_t.h, s_t.n)
            emit_ln_chunk(block, nc,
                          lambda b, c=c: x3[:, 4 * c + b, :],
                          lambda b: xn_bf[:, b * D:(b + 1) * D],
                          stats, s_v, s_sc, nblk=4, pre_wait=pre)

            @block.tensor
            def _(e):
                e.wait_ge(s_v.h, s_v.n)
                if not eye_wait_done[0]:
                    e.wait_ge(s_x.h, s_x.n)
                    eye_wait_done[0] = True
            emit_transpose_chunk(
                block, nc,
                lambda g, j, c=c: xn_bf[:, j * D + g * 128: j * D + g * 128 + 128],
                lambda g, c=c: xn_fm[:].rearrange("p (k t) -> p k t", t=T)[:, g, c * 512:(c + 1) * 512],
                eye_sb, ps_t, s_t, s_sc, ngroups=8)

            tp_done_count = s_sc.n
            for ft in range(NF):
                gi = c * NF + ft
                bank = ft % 2


                @block.tensor
                def _(e, ft=ft, gi=gi, bank=bank, tp=tp_done_count):
                    e.wait_ge(s_sc.h, tp)
                    e.wait_ge(s_w1[gi % 2].h, w1_sent[gi])
                    if gi >= 2:
                        e.wait_ge(s_sc.h, h_act_acks[gi - 2])
                    for dt in range(8):
                        ins = e.matmul(ps_h[bank][:, :],
                                       w1v[:, gi % 2, dt, :],
                                       xn_fm[:].rearrange("p (k t) -> p k t", t=T)[:, dt, c * 512:(c + 1) * 512],
                                       start=(dt == 0), stop=(dt == 7))
                    h_group_acks.append(s_t.inc(ins, 1))

                @block.gpsimd
                def _(e, gi=gi):
                    nxt = gi + 2
                    if nxt < NCH * NF:
                        cc, fft = divmod(nxt, NF)
                        e.wait_ge(s_t.h, h_group_acks[gi])
                        w1_sent.append(
                            s_w1[nxt % 2].inc(e.dma_start(out=w1v[:, nxt % 2, :, :],
                                                          in_=w1_src[:, :, fft * 128:(fft + 1) * 128]), 16))

                @block.scalar
                def _(e, ft=ft, gi=gi, bank=bank):
                    e.wait_ge(s_t.h, h_group_acks[gi])
                    if gi == 0:
                        e.wait_ge(s_x.h, s_x.n)
                    if c > 0 and ft == 0:
                        e.wait_ge(s_t.h, y_group_acks_last[0])
                    ins = e.activation(h_bf[:, ft * 512:(ft + 1) * 512],
                                       ps_h[bank][:, :], AF.Relu,
                                       bias=b1_sb[:, ft:ft + 1])
                    h_act_acks.append(s_sc.inc(ins, 1))

            h_done = s_sc.n
            y_group_acks_last = [None]
            for blk in range(4):
                for dc in range(2):
                    gj = (c * 4 + blk) * 2 + dc
                    bank = gj % 2

                    @block.tensor
                    def _(e, blk=blk, dc=dc, gj=gj, bank=bank, hd=h_done):
                        e.wait_ge(s_sc.h, hd)
                        if gj == 0:
                            e.wait_ge(s_w.h, 80)
                            e.wait_ge(s_v.h, 1)
                        if gj >= 2:
                            e.wait_ge(s_v.h, y_evac_acks[gj - 2])
                        h3 = h_bf[:].rearrange("p (k t) -> p k t", t=512)
                        w23 = w2_bf[:].rearrange("p (k d) -> p k d", d=D)
                        for ft in range(NF):
                            e.matmul(ps_y[bank][:, :], h3[:, ft, blk * 128:(blk + 1) * 128],
                                     w23[:, ft, dc * 512:(dc + 1) * 512],
                                     start=(ft == 0), stop=False)
                        ins = e.matmul(ps_y[bank][:, :], ones1[:, :],
                                       b2_sb[:, dc * 512:(dc + 1) * 512],
                                       start=False, stop=True)
                        y_group_acks_last[0] = s_t.inc(ins, 1)

                    @block.vector
                    def _(e, blk=blk, dc=dc, gj=gj, bank=bank, c=c):
                        e.wait_ge(s_t.h, s_t.n)
                        dst = x3[:, 4 * c + blk, dc * 512:(dc + 1) * 512]
                        ins = e.scalar_tensor_tensor(dst, ps_y[bank][:, :], 0.5,
                                                     dst, ALU.mult, ALU.add)
                        y_evac_acks.append(s_v.inc(ins, 1))

            @block.sync
            def _(e, c=c):
                e.wait_ge(s_v.h, s_v.n)
                s_st.inc(e.dma_start(out=y_dst[:, 4 * c:4 * (c + 1), :],
                                     in_=x3[:, 4 * c:4 * (c + 1), :]), 16)

        @block.sync
        def _(e):
            e.wait_ge(s_st.h, s_st.n)
    return nc



def _eye_bf():
    return np.eye(128, dtype=BFNP)


def run_mlp_launch(x_full, w1, b1, w2, b2):
    T = x_full.shape[0] // NCORE
    F = w1.shape[1]
    nc = build_mlp(T=T, F=F)
    b1c = np.ascontiguousarray(b1.reshape(F // 128, 128).T)
    b2r = np.ascontiguousarray(b2.reshape(1, D))
    eye = _eye_bf()
    in_maps = []
    for c in range(NCORE):
        in_maps.append({
            "x": np.ascontiguousarray(x_full[c * T:(c + 1) * T]),
            "w1": w1, "b1c": b1c, "w2": w2, "b2r": b2r, "eye": eye,
        })
    res = run_bass_kernel_spmd(nc, in_maps, core_ids=list(range(NCORE)))
    return np.concatenate([res.results[c]["y"] for c in range(NCORE)], axis=0)



def build_proj(T=2048, HV=1024, has_bias=False):
    nc = bass.Bass("TRN2", target_bir_lowering=False)
    NB, NH = T // 128, HV // 128
    x_in = nc.dram_tensor("x", [T, D], F32, kind="ExternalInput").ap()
    o_in = nc.dram_tensor("ofm", [HV, T], F32, kind="ExternalInput").ap()
    wo_in = nc.dram_tensor("wo", [HV, D], F32, kind="ExternalInput").ap()
    if has_bias:
        bo_in = nc.dram_tensor("bor", [1, D], F32, kind="ExternalInput").ap()
    y_out = nc.dram_tensor("y", [T, D], F32, kind="ExternalOutput").ap()

    x_src = x_in.rearrange("(t p) d -> p t d", p=128)
    y_dst = y_out.rearrange("(t p) d -> p t d", p=128)
    o_src = o_in.rearrange("(k p) t -> p k t", p=128)
    wo_src = wo_in.rearrange("(k p) d -> p k d", p=128)

    with ExitStack() as ctx:
        sb = lambda name, shape, dt: ctx.enter_context(nc.sbuf_tensor(name, shape, dt))
        ps = lambda name, shape, dt: ctx.enter_context(nc.psum_tensor(name, shape, dt))
        sem = lambda name: Sem(ctx.enter_context(nc.semaphore(name)))
        x_sb = sb("x_sb", [128, NB * D], F32)
        o_bf = sb("o_bf", [128, NH * T], BF16)
        wo_bf = sb("wo_bf", [128, NH * D], BF16)
        bo_sb = sb("bo_sb", [1, D], BF16)
        ones1 = sb("ones1", [1, 128], BF16)
        ps_y0 = ps("ps_y0", [128, 512], F32)
        ps_y1 = ps("ps_y1", [128, 512], F32)
        s_x, s_w, s_v, s_t, s_st = (sem(n) for n in
                                    ("s_x", "s_w", "s_v", "s_t", "s_st"))
        block = ctx.enter_context(nc.Block())
        ps_y = [ps_y0, ps_y1]
        x3 = x_sb[:].rearrange("p (t d) -> p t d", d=D)
        o3 = o_bf[:].rearrange("p (k t) -> p k t", t=T)
        wo3 = wo_bf[:].rearrange("p (k d) -> p k d", d=D)

        @block.sync
        def _(e):
            s_x.inc(e.dma_start(out=x3[:, :, :], in_=x_src[:, :, :]), 16)

        @block.gpsimd
        def _(e):
            if has_bias:
                s_w.inc(e.dma_start(out=bo_sb[:, :], in_=bo_in[:, :]), 16)
            for i in range(2):
                hh = NH // 2
                s_w.inc(e.dma_start(out=o3[:, hh * i:hh * (i + 1), :],
                                    in_=o_src[:, hh * i:hh * (i + 1), :]), 16)
                s_w.inc(e.dma_start(out=wo3[:, hh * i:hh * (i + 1), :],
                                    in_=wo_src[:, hh * i:hh * (i + 1), :]), 16)

        @block.vector
        def _(e):
            s_v.inc(e.memset(ones1[:, :], 1.0), 1)

        y_evac_acks = []
        for blk in range(NB):
            for dc in range(2):
                gj = blk * 2 + dc
                bank = gj % 2

                @block.tensor
                def _(e, blk=blk, dc=dc, gj=gj, bank=bank):
                    if gj == 0:
                        e.wait_ge(s_w.h, s_w.n)
                        e.wait_ge(s_v.h, 1)
                    if gj >= 2:
                        e.wait_ge(s_v.h, y_evac_acks[gj - 2])
                    for k in range(NH):
                        ins = e.matmul(ps_y[bank][:, :],
                                       o3[:, k, blk * 128:(blk + 1) * 128],
                                       wo3[:, k, dc * 512:(dc + 1) * 512],
                                       start=(k == 0), stop=(not has_bias and k == NH - 1))
                    if has_bias:
                        ins = e.matmul(ps_y[bank][:, :], ones1[:, :],
                                       bo_sb[:, dc * 512:(dc + 1) * 512],
                                       start=False, stop=True)
                    s_t.inc(ins, 1)

                @block.vector
                def _(e, blk=blk, dc=dc, gj=gj, bank=bank):
                    e.wait_ge(s_t.h, s_t.n)
                    if gj == 0:
                        e.wait_ge(s_x.h, 16)
                    dst = x3[:, blk, dc * 512:(dc + 1) * 512]
                    ins = e.scalar_tensor_tensor(dst, ps_y[bank][:, :], 1.0,
                                                 dst, ALU.mult, ALU.add)
                    y_evac_acks.append(s_v.inc(ins, 1))

            @block.sync
            def _(e, blk=blk):
                e.wait_ge(s_v.h, s_v.n)
                s_st.inc(e.dma_start(out=y_dst[:, blk, :], in_=x3[:, blk, :]), 16)

        @block.sync
        def _(e):
            e.wait_ge(s_st.h, s_st.n)
    return nc


def run_proj_launch(x_full, o_fm_full, wo, bo=None):
    T = 2048
    HV = o_fm_full[0].shape[0]
    nc = build_proj(T=T, HV=HV, has_bias=bo is not None)
    in_maps = []
    for c in range(NCORE):
        b, h = c // 2, c % 2
        im = {
            "x": np.ascontiguousarray(x_full[c * T:(c + 1) * T]),
            "ofm": np.ascontiguousarray(o_fm_full[b][:, h * T:(h + 1) * T]),
            "wo": wo,
        }
        if bo is not None:
            im["bor"] = np.ascontiguousarray(bo.reshape(1, D))
        in_maps.append(im)
    res = run_bass_kernel_spmd(nc, in_maps, core_ids=list(range(NCORE)))
    return np.concatenate([res.results[c]["y"] for c in range(NCORE)], axis=0)



def build_l0attn(S=4096, NHEAD=8, dbg_stop=0):
    nc = bass.Bass("TRN2", target_bir_lowering=False)
    NCH = S // 512
    NBLK = S // 128
    HV = NHEAD * 64
    EPS = 1e-6
    x_in = nc.dram_tensor("x", [S, D], F32, kind="ExternalInput").ap()
    wq_in = nc.dram_tensor("wqc", [128, HV], F32, kind="ExternalInput").ap()
    wk_in = nc.dram_tensor("wkc", [128, HV], F32, kind="ExternalInput").ap()
    wv_in = nc.dram_tensor("wvc", [128, HV], F32, kind="ExternalInput").ap()
    eye_in = nc.dram_tensor("eye", [128, 128], BF16, kind="ExternalInput").ap()
    ctx_out = nc.dram_tensor("ctx", [HV, S], F32, kind="ExternalOutput").ap()

    x_src = x_in.rearrange("(t p) d -> p t d", p=128)

    with ExitStack() as ctx:
        sb = lambda name, shape, dt: ctx.enter_context(nc.sbuf_tensor(name, shape, dt))
        ps = lambda name, shape, dt: ctx.enter_context(nc.psum_tensor(name, shape, dt))
        sem = lambda name: Sem(ctx.enter_context(nc.semaphore(name)))
        x_st = sb("x_st", [128, 2 * 4 * D], F32)
        xn_bf = sb("xn_bf", [128, 4 * D], BF16)
        xn_fm = sb("xn_fm", [128, 8 * S], BF16)
        qf_bf = sb("qf_bf", [128, 4 * S], BF16)
        kf_bf = sb("kf_bf", [128, 2 * 512], BF16)
        vmin_t = sb("vmin_t", [128, 512], F32)
        vexp_t = sb("vexp_t", [128, 512], F32)
        v_aug = sb("v_aug", [128, NHEAD * NBLK * 65], BF16)
        kv_bf = sb("kv_bf", [128, NHEAD * 65], BF16)
        den_t = sb("den_t", [1, 512], F32)
        num_t = sb("num_t", [64, 2 * 512], BF16)
        ctx_sb = sb("ctx_sb", [64, 2 * S], BF16)
        wq_sb = sb("wq_sb", [128, HV], BF16)
        wk_sb = sb("wk_sb", [128, HV], BF16)
        wv_sb = sb("wv_sb", [128, HV], BF16)
        ones1 = sb("ones1", [1, 64], F32)
        eye_sb = sb("eye_sb", [128, 128], BF16)
        stats = sb("stats", [128, 64], F32)
        ps0 = ps("ps0", [128, 512], F32)
        ps1 = ps("ps1", [128, 512], F32)
        ps2 = ps("ps2", [128, 512], F32)
        ps3 = ps("ps3", [128, 512], F32)
        inner_ps = ExitStack()
        ps_t0 = inner_ps.enter_context(nc.psum_tensor("ps_t0", [128, 512], BF16))
        ps_t1 = inner_ps.enter_context(nc.psum_tensor("ps_t1", [128, 512], BF16))
        s_xa, s_xb = sem("s_xa"), sem("s_xb")
        s_w, s_v, s_sc, s_t = sem("s_w"), sem("s_v"), sem("s_sc"), sem("s_t")
        s_sta, s_stb = sem("s_sta"), sem("s_stb")
        block = ctx.enter_context(nc.Block())
        s_x = [s_xa, s_xb]
        s_st = [s_sta, s_stb]
        ps_t = [ps_t0, ps_t1]
        x3 = x_st[:].rearrange("p (c t d) -> p c t d", c=2, d=D)
        xnf = xn_fm[:].rearrange("p (k t) -> p k t", t=S)
        qf3 = qf_bf[:].rearrange("p (k t) -> p k t", t=S)
        va4 = v_aug[:].rearrange("p (h b f) -> p h b f", h=NHEAD, f=65)
        ctx3 = ctx_sb[:].rearrange("p (b t) -> p b t", b=2)

        def xn_head(h, c0, c1):
            a = xnf[:, h // 2, c0:c1]
            return a[(h % 2) * 64:(h % 2) * 64 + 64, :]

        @block.sync
        def _(e):
            s_xa.inc(e.dma_start(out=x3[:, 0, :, :], in_=x_src[:, 0:4, :]), 16)
            s_xb.inc(e.dma_start(out=x3[:, 1, :, :], in_=x_src[:, 4:8, :]), 16)

        @block.gpsimd
        def _(e):
            s_w.inc(e.dma_start(out=wq_sb[:, :], in_=wq_in[:, :]), 16)
            s_w.inc(e.dma_start(out=wk_sb[:, :], in_=wk_in[:, :]), 16)
            s_w.inc(e.dma_start(out=wv_sb[:, :], in_=wv_in[:, :]), 16)
            s_w.inc(e.dma_start(out=eye_sb[:, :], in_=eye_in[:, :]), 16)

        @block.vector
        def _(e):
            e.memset(ones1[:, :], 1.0)
            s_v.inc(e.memset(v_aug[:, :], 1.0), 1)

        ln_acks = []
        for c in range(NCH):
            def pre(e, c=c):
                e.wait_ge(s_x[c % 2].h, 16 * (c // 2 + 1))
                if c > 0:
                    e.wait_ge(s_t.h, s_t.n)
            emit_ln_chunk(block, nc,
                          lambda b, c=c: x3[:, c % 2, b, :],
                          lambda b: xn_bf[:, b * D:(b + 1) * D],
                          stats, s_v, s_sc, nblk=4, pre_wait=pre)
            ln_acks.append(s_v.n)

            @block.sync
            def _(e, c=c):
                if c + 2 < NCH:
                    e.wait_ge(s_v.h, ln_acks[c])
                    s_x[c % 2].inc(e.dma_start(out=x3[:, c % 2, :, :],
                                               in_=x_src[:, 4 * (c + 2):4 * (c + 3), :]), 16)

            @block.tensor
            def _(e, c=c):
                e.wait_ge(s_v.h, ln_acks[c])
                if c == 0:
                    e.wait_ge(s_w.h, 64)
            emit_transpose_chunk(
                block, nc,
                lambda g, j, c=c: xn_bf[:, j * D + g * 128: j * D + g * 128 + 128],
                lambda g, c=c: xnf[:, g, c * 512:(c + 1) * 512],
                eye_sb, ps_t, s_t, s_sc, ngroups=8)
        xn_ready = s_sc.n
        if dbg_stop == 1:
            @block.gpsimd
            def _(e):
                e.wait_ge(s_sc.h, xn_ready)
                s_sta.inc(e.dma_start(
                    out=ctx_out.rearrange("(a p) s -> p a s", p=128),
                    in_=xn_fm[:].rearrange("p (k t) -> p k t", t=S)[:, 0:4, :]), 16)
                e.wait_ge(s_sta.h, s_sta.n)
            inner_ps.close()
            return nc
        inner_ps.close()
        ps_kv_a = ps("ps_kv_a", [64, NHEAD // 2 * 65], F32)
        ps_kv_b = ps("ps_kv_b", [64, NHEAD // 2 * 65], F32)
        ps_bc = ps("ps_bc", [64, 512], F32)

        pk_e, pk_o, pv_e, pv_o = ps0, ps1, ps2, ps3
        kraw = vmin_t
        kf_acks = {}
        kv_mm_acks = {}
        cp_acks = {}
        for i in range(NBLK):
            @block.tensor
            def _(e, i=i):
                if i == 0:
                    e.wait_ge(s_sc.h, xn_ready)
                    e.wait_ge(s_w.h, 64)
                if i >= 1:
                    e.wait_ge(s_v.h, cp_acks[i - 1])
                for h in range(NHEAD):
                    pk = pk_e if h % 2 == 0 else pk_o
                    e.matmul(pk[:, (h // 2) * 64:(h // 2 + 1) * 64],
                             xn_head(h, i * 128, (i + 1) * 128),
                             wk_sb[(h % 2) * 64:(h % 2) * 64 + 64, h * 64:(h + 1) * 64],
                             start=True, stop=True)
                for h in range(NHEAD):
                    pv = pv_e if h % 2 == 0 else pv_o
                    ins = e.matmul(pv[:, (h // 2) * 64:(h // 2 + 1) * 64],
                                   xn_head(h, i * 128, (i + 1) * 128),
                                   wv_sb[(h % 2) * 64:(h % 2) * 64 + 64, h * 64:(h + 1) * 64],
                                   start=True, stop=True)
                s_t.inc(ins, 1)

            @block.vector
            def _(e, i=i):
                e.wait_ge(s_t.h, s_t.n)
                if i == 0:
                    e.wait_ge(s_v.h, 1)
                e.tensor_copy(va4[:, 0:NHEAD:2, i, 0:64],
                              pv_e[:, 0:256].rearrange("p (h f) -> p h f", f=64))
                e.tensor_copy(va4[:, 1:NHEAD:2, i, 0:64],
                              pv_o[:, 0:256].rearrange("p (h f) -> p h f", f=64))
                e.tensor_copy(kraw[:, 0:256], pk_e[:, 0:256])
                ins = e.tensor_copy(kraw[:, 256:512], pk_o[:, 0:256])
                cp_acks[i] = s_v.inc(ins, 1)
                e.wait_ge(s_v.h, s_v.n)
                s_v.inc(e.tensor_scalar_min(vexp_t[:, :], kraw[:, :], 0.0), 1)

            @block.scalar
            def _(e, i=i):
                e.wait_ge(s_v.h, s_v.n)
                s_sc.inc(e.activation(vexp_t[:, :], vexp_t[:, :], AF.Exp), 1)

            @block.vector
            def _(e, i=i):
                e.wait_ge(s_sc.h, s_sc.n)
                if i >= 2:
                    e.wait_ge(s_t.h, kv_mm_acks[i - 2])
                ins = e.scalar_tensor_tensor(
                    kf_bf[:, (i % 2) * 512:(i % 2 + 1) * 512],
                    kraw[:, :], 0.0, vexp_t[:, :], ALU.max, ALU.add)
                kf_acks[i] = s_v.inc(ins, 1)

            @block.tensor
            def _(e, i=i):
                e.wait_ge(s_v.h, kf_acks[i])
                for h in range(NHEAD):
                    pkv = ps_kv_a if h < NHEAD // 2 else ps_kv_b
                    hh2 = h % (NHEAD // 2)
                    koff = (i % 2) * 512 + (h % 2) * 256 + (h // 2) * 64
                    ins = e.matmul(pkv[:, hh2 * 65:(hh2 + 1) * 65],
                                   kf_bf[:, koff:koff + 64],
                                   va4[:, h, i, :],
                                   start=(i == 0 and h in (0, NHEAD // 2)),
                                   stop=(i == NBLK - 1 and h in (NHEAD // 2 - 1, NHEAD - 1)))
                kv_mm_acks[i] = s_t.inc(ins, 1)

        @block.scalar
        def _(e):
            e.wait_ge(s_t.h, s_t.n)
            half = NHEAD // 2 * 65
            e.copy(kv_bf[0:64, 0:half], ps_kv_a[:, :])
            e.copy(kv_bf[0:64, half:2 * half], ps_kv_b[:, :])
            e.copy(kv_bf[64:128, 0:half], ps_kv_a[:, :])
            s_sc.inc(e.copy(kv_bf[64:128, half:2 * half], ps_kv_b[:, :]), 1)
        kv_ready = s_sc.n
        if dbg_stop == 2:
            @block.gpsimd
            def _(e):
                e.wait_ge(s_sc.h, kv_ready)
                s_sta.inc(e.dma_start(out=ctx_out[0:128, 0:NHEAD * 65],
                                      in_=kv_bf[:, :]), 16)
                e.wait_ge(s_sta.h, s_sta.n)
            return nc

        q_acks = []
        qi = 0
        for c in range(NCH):
            for hp in range(4):
                pq = [ps0, ps1][qi % 2]

                @block.tensor
                def _(e, c=c, hp=hp, pq=pq, qi=qi):
                    if qi >= 2:
                        e.wait_ge(s_v.h, q_acks[qi - 2])
                    for sub in range(2):
                        h = hp * 2 + sub
                        base = (h % 2) * 64
                        ins = e.matmul(pq[sub * 64:(sub + 1) * 64, :],
                                       wq_sb[base:base + 64, h * 64:(h + 1) * 64],
                                       xn_head(h, c * 512, (c + 1) * 512),
                                       start=True, stop=True)
                    s_t.inc(ins, 1)

                @block.vector
                def _(e, pq=pq):
                    e.wait_ge(s_t.h, s_t.n)
                    s_v.inc(e.tensor_scalar_min(vmin_t[:, :], pq[:, :], 0.0), 1)

                @block.scalar
                def _(e):
                    e.wait_ge(s_v.h, s_v.n)
                    s_sc.inc(e.activation(vexp_t[:, :], vmin_t[:, :], AF.Exp), 1)

                @block.vector
                def _(e, pq=pq, c=c, hp=hp):
                    e.wait_ge(s_sc.h, s_sc.n)
                    ins = e.scalar_tensor_tensor(qf3[:, hp, c * 512:(c + 1) * 512],
                                                 pq[:, :], 0.0, vexp_t[:, :],
                                                 ALU.max, ALU.add)
                    q_acks.append(s_v.inc(ins, 1))
                qi += 1

        if dbg_stop == 3:
            @block.gpsimd
            def _(e):
                e.wait_ge(s_v.h, s_v.n)
                s_sta.inc(e.dma_start(
                    out=ctx_out.rearrange("(a p) s -> p a s", p=128),
                    in_=qf_bf[:].rearrange("p (k t) -> p k t", t=S)[:, 0:4, :]), 16)
                e.wait_ge(s_sta.h, s_sta.n)
            return nc

        nd_acks = []
        st_acks = {}
        ni = 0
        for h in range(NHEAD):
            for c in range(NCH):
                pn = [ps2, ps3][ni % 2]

                @block.tensor
                def _(e, h=h, c=c, pn=pn, ni=ni):
                    if ni == 0:
                        e.wait_ge(s_sc.h, kv_ready)
                        e.wait_ge(s_v.h, s_v.n)
                    if ni >= 2:
                        e.wait_ge(s_v.h, nd_acks[ni - 2])
                    qf_h = qf3[:, h // 2, c * 512:(c + 1) * 512]
                    qf_h = qf_h[(h % 2) * 64:(h % 2) * 64 + 64, :]
                    base = (h % 2) * 64
                    ins = e.matmul(pn[0:65, :],
                                   kv_bf[base:base + 64, h * 65:(h + 1) * 65],
                                   qf_h, start=True, stop=True)
                    s_t.inc(ins, 1)
                ndmm_ack = s_t.n

                @block.vector
                def _(e, pn=pn):
                    e.wait_ge(s_t.h, ndmm_ack)
                    s_v.inc(e.tensor_scalar_add(den_t[:, :], pn[64:65, :], EPS), 1)
                    e.wait_ge(s_v.h, s_v.n)
                    s_v.inc(e.reciprocal(den_t[:, :], den_t[:, :]), 1)
                recip_ack = s_v.n

                @block.scalar
                def _(e, pn=pn, ni=ni):
                    e.wait_ge(s_t.h, ndmm_ack)
                    if ni >= 2:
                        e.wait_ge(s_v.h, nd_acks[ni - 2])
                    s_sc.inc(e.copy(num_t[:, (ni % 2) * 512:(ni % 2 + 1) * 512],
                                    pn[0:64, :]), 1)
                numcp_ack = s_sc.n

                @block.tensor
                def _(e, ni=ni):
                    e.wait_ge(s_v.h, recip_ack)
                    if ni >= 1:
                        e.wait_ge(s_v.h, nd_acks[ni - 1])
                    ins = e.matmul(ps_bc[:, :], ones1[:, :], den_t[:, :],
                                   start=True, stop=True)
                    s_t.inc(ins, 1)
                bc_ack = s_t.n

                @block.vector
                def _(e, h=h, c=c, ni=ni):
                    e.wait_ge(s_t.h, bc_ack)
                    e.wait_ge(s_sc.h, numcp_ack)
                    if h >= 2 and (h - 2, 0) in st_acks:
                        e.wait_ge(s_st[h % 2].h, st_acks[(h - 2, 0)])
                    ins = e.tensor_tensor(ctx3[:, h % 2, c * 512:(c + 1) * 512],
                                          num_t[:, (ni % 2) * 512:(ni % 2 + 1) * 512],
                                          ps_bc[:, :], ALU.mult)
                    nd_acks.append(s_v.inc(ins, 1))
                ni += 1

            @block.gpsimd
            def _(e, h=h):
                e.wait_ge(s_v.h, s_v.n)
                st_acks[(h, 0)] = s_st[h % 2].inc(
                    e.dma_start(out=ctx_out[h * 64:(h + 1) * 64, :],
                                in_=ctx3[:, h % 2, :]), 16)

        @block.gpsimd
        def _(e):
            e.wait_ge(s_sta.h, s_sta.n)
            e.wait_ge(s_stb.h, s_stb.n)
    return nc


def run_l0attn_launch(x_full, Wq, Wk, Wv):
    S = x_full.shape[1]
    eye = _eye_bf()
    in_maps = []
    nc = build_l0attn(S=S, NHEAD=8)
    for c in range(NCORE):
        b, hh = c // 2, c % 2
        heads = range(hh * 8, hh * 8 + 8)
        wqc = np.concatenate([Wq[h] for h in heads], axis=1)
        wkc = np.concatenate([Wk[h] for h in heads], axis=1)
        wvc = np.concatenate([Wv[h] for h in heads], axis=1)
        wqc = np.concatenate([wqc, wqc], axis=0)
        wkc = np.concatenate([wkc, wkc], axis=0)
        wvc = np.concatenate([wvc, wvc], axis=0)
        xb = np.roll(x_full[b], -hh * 512, axis=1)
        in_maps.append({"x": np.ascontiguousarray(xb),
                        "wqc": np.ascontiguousarray(wqc),
                        "wkc": np.ascontiguousarray(wkc),
                        "wvc": np.ascontiguousarray(wvc), "eye": eye})
    res = run_bass_kernel_spmd(nc, in_maps, core_ids=list(range(NCORE)))
    out = []
    for b in range(4):
        out.append(np.concatenate([res.results[2 * b]["ctx"],
                                   res.results[2 * b + 1]["ctx"]], axis=0))
    return out



def build_causal_attn(S=4096, NHEAD=8, mode="l1"):
    nc = bass.Bass("TRN2", target_bir_lowering=False)
    NCH = S // 512
    NBLK = S // 128
    HV = NHEAD * 64
    NG = 4
    HPG = NHEAD // NG
    scale = 64.0 ** -0.5
    x_in = nc.dram_tensor("x", [S, D], F32, kind="ExternalInput").ap()
    wq_in = nc.dram_tensor("wq", [D, HV], F32, kind="ExternalInput").ap()
    wk_in = nc.dram_tensor("wk", [D, HV], F32, kind="ExternalInput").ap()
    wv_in = nc.dram_tensor("wv", [D, HV], F32, kind="ExternalInput").ap()
    if mode == "l1":
        gbc_in = nc.dram_tensor("gbc", [128, D], F32, kind="ExternalInput").ap()
    else:
        ropec_in = nc.dram_tensor("ropec", [128, S], F32, kind="ExternalInput").ap()
        ropes_in = nc.dram_tensor("ropes", [128, S], F32, kind="ExternalInput").ap()
        pswap_in = nc.dram_tensor("pswap", [128, 128], BF16, kind="ExternalInput").ap()
    maskq_in = nc.dram_tensor("maskq", [4, 128, 512], F32, kind="ExternalInput").ap()
    maskk_in = nc.dram_tensor("maskk", [4, 128, 512], F32, kind="ExternalInput").ap()
    eye_in = nc.dram_tensor("eye", [128, 128], BF16, kind="ExternalInput").ap()
    o_out = nc.dram_tensor("o", [HV, S], F32, kind="ExternalOutput").ap()

    x_src = x_in.rearrange("(t p) d -> p t d", p=128)

    with ExitStack() as ctx:
        sb = lambda name, shape, dt: ctx.enter_context(nc.sbuf_tensor(name, shape, dt))
        ps = lambda name, shape, dt: ctx.enter_context(nc.psum_tensor(name, shape, dt))
        sem = lambda name: Sem(ctx.enter_context(nc.semaphore(name)))
        xn_fm = sb("xn_fm", [128, 8 * S], BF16)
        q_fm = sb("q_fm", [128, (HPG // 2) * S], BF16)
        k_fm = sb("k_fm", [128, (HPG // 2) * S], BF16)
        va4_t = sb("va4", [128, HPG * NBLK * 65], BF16)
        o_fm = sb("o_fm", [128, (HPG // 2) * S], BF16)
        k_aug = sb("k_aug", [65, S], BF16)
        q_aug = sb("q_aug", [65, S], BF16)
        m_all = sb("m_all", [128, 40], F32)
        m_bf = sb("m_bf", [128, 32], BF16)
        m_row = sb("m_row", [32, 128], BF16)
        tmpf = sb("tmpf", [128, 512], F32)
        tmpg = sb("tmpg", [128, 512], F32)
        p_sb = sb("p_sb", [128, 2 * 512], BF16)
        rden = sb("rden", [1, 512], F32)
        onum = sb("onum", [64, 512], BF16)
        maskq_sb = sb("maskq_sb", [128, 4 * 512], F32)
        maskk_sb = sb("maskk_sb", [128, 4 * 512], F32)
        ones64 = sb("ones64", [1, 64], F32)
        eye_sb = sb("eye_sb", [128, 128], BF16)
        stats = sb("stats", [128, 64], F32)
        wq_sb = sb("wq_sb", [128, 8 * (HPG * 64)], BF16)
        wk_sb = sb("wk_sb", [128, 8 * (HPG * 64)], BF16)
        wv_sb = sb("wv_sb", [128, 8 * (HPG * 64)], BF16)
        if mode == "l1":
            gbc_sb = sb("gbc_sb", [128, D], F32)
        else:
            ropec_sb = sb("ropec_sb", [128, S], F32)
            ropes_sb = sb("ropes_sb", [128, S], F32)
            pswap_sb = sb("pswap_sb", [128, 128], BF16)
        x_st = sb("x_st", [128, 4 * D], F32)
        xn_bf = sb("xn_bf", [128, 4 * D], BF16)
        if mode == "l1":
            xn1 = sb("xn1", [128, 4 * D], F32)
        ps0 = ps("ps0", [128, 512], F32)
        ps1 = ps("ps1", [128, 512], F32)
        ps2 = ps("ps2", [128, 512], F32)
        ps3 = ps("ps3", [128, 512], F32)
        ps_o = ps("ps_o", [65, 512], F32)
        ps_bc = ps("ps_bc", [64, 512], F32)
        ps_tm = ps("ps_tm", [128, 512], BF16)
        ps_t1 = ps("ps_t1", [128, 512], BF16)
        s_xa, s_xb = sem("s_xa"), sem("s_xb")
        s_w, s_v, s_sc, s_t = sem("s_w"), sem("s_v"), sem("s_sc"), sem("s_t")
        s_wa, s_wb = sem("s_wa"), sem("s_wb")
        s_g = sem("s_g")
        s_st = sem("s_st")
        block = ctx.enter_context(nc.Block())
        s_x = [s_xa, s_xb]
        ps_t = [ps_tm, ps_t1]
        x3 = x_st[:].rearrange("p (t d) -> p t d", d=D)
        xnf = xn_fm[:].rearrange("p (k t) -> p k t", t=S)
        qf3 = q_fm[:].rearrange("p (k t) -> p k t", t=S)
        kf3 = k_fm[:].rearrange("p (k t) -> p k t", t=S)
        of3 = o_fm[:].rearrange("p (k t) -> p k t", t=S)
        va4 = va4_t[:].rearrange("p (h b f) -> p h b f", h=HPG, f=65)
        mq3 = maskq_sb[:].rearrange("p (a c) -> p a c", a=4)
        mk3 = maskk_sb[:].rearrange("p (a c) -> p a c", a=4)
        wq3 = wq_sb[:].rearrange("p (k c) -> p k c", k=8)
        wk3 = wk_sb[:].rearrange("p (k c) -> p k c", k=8)
        wv3 = wv_sb[:].rearrange("p (k c) -> p k c", k=8)

        @block.sync
        def _(e):
            s_xa.inc(e.dma_start(out=x3[:, 0:4, :], in_=x_src[:, 0:4, :]), 16)

        @block.gpsimd
        def _(e):
            s_w.inc(e.dma_start(out=eye_sb[:, :], in_=eye_in[:, :]), 16)
            for a in range(4):
                s_w.inc(e.dma_start(out=mq3[:, a, :], in_=maskq_in[a, :, :]), 16)
                s_w.inc(e.dma_start(out=mk3[:, a, :], in_=maskk_in[a, :, :]), 16)
            if mode == "l1":
                s_w.inc(e.dma_start(out=gbc_sb[:, :], in_=gbc_in[:, :]), 16)
            else:
                s_w.inc(e.dma_start(out=ropec_sb[:, :], in_=ropec_in[:, :]), 16)
                s_w.inc(e.dma_start(out=ropes_sb[:, :], in_=ropes_in[:, :]), 16)
                s_w.inc(e.dma_start(out=pswap_sb[:, :], in_=pswap_in[:, :]), 16)
        prologue_w = s_w.n

        @block.vector
        def _(e):
            e.memset(ones64[:, :], 1.0)
            e.memset(k_aug[:, :], 1.0)
            e.memset(va4_t[:, :], 1.0)
            s_v.inc(e.memset(q_aug[:, :], 1.0), 1)

        ln_acks = []
        for c in range(NCH):
            def pre(e, c=c):
                e.wait_ge(s_xa.h, 16 * (c + 1))
                if c > 0:
                    e.wait_ge(s_t.h, s_t.n)
            if mode == "l1":
                emit_ln_chunk(block, nc,
                              lambda b, c=c: x3[:, b, :],
                              lambda b: xn1[:, b * D:(b + 1) * D],
                              stats, s_v, s_sc, nblk=4, pre_wait=pre)
                emit_ln_chunk(block, nc,
                              lambda b: xn1[:, b * D:(b + 1) * D],
                              lambda b: xn1[:, b * D:(b + 1) * D],
                              stats, s_v, s_sc, nblk=4,
                              pre_wait=lambda e: e.wait_ge(s_v.h, s_v.n))

                @block.vector
                def _(e, c=c):
                    if c == 0:
                        e.wait_ge(s_w.h, prologue_w)
                    e.wait_ge(s_v.h, s_v.n)
                    for b in range(4):
                        ins = e.tensor_tensor(xn_bf[:, b * D:(b + 1) * D],
                                              xn1[:, b * D:(b + 1) * D],
                                              gbc_sb[:, :], ALU.mult)
                    s_v.inc(ins, 1)
            else:
                emit_ln_chunk(block, nc,
                              lambda b, c=c: x3[:, b, :],
                              lambda b: xn_bf[:, b * D:(b + 1) * D],
                              stats, s_v, s_sc, nblk=4, pre_wait=pre)
            ln_acks.append(s_v.n)

            @block.sync
            def _(e, c=c):
                if c + 1 < NCH:
                    e.wait_ge(s_v.h, ln_acks[c])
                    s_xa.inc(e.dma_start(out=x3[:, 0:4, :],
                                         in_=x_src[:, 4 * (c + 1):4 * (c + 2), :]), 16)

            @block.tensor
            def _(e, c=c):
                e.wait_ge(s_v.h, ln_acks[c])
                if c == 0:
                    e.wait_ge(s_w.h, prologue_w)
            emit_transpose_chunk(
                block, nc,
                lambda g, j, c=c: xn_bf[:, j * D + g * 128: j * D + g * 128 + 128],
                lambda g, c=c: xnf[:, g, c * 512:(c + 1) * 512],
                eye_sb, ps_t, s_t, s_sc, ngroups=8)
        xn_ready = s_sc.n

        bank_free = {}

        def use_bank(e, key):
            if key in bank_free:
                sm, n = bank_free[key]
                e.wait_ge(sm.h, n)

        def set_bank(key, sm, n):
            bank_free[key] = (sm, n)

        prev_ostore = [0]
        for grp in range(NG):
            gh0 = grp * HPG
            wsem = [s_wa, s_wb][grp % 2]

            @block.gpsimd
            def _(e, grp=grp, gh0=gh0, wsem=wsem):
                if grp > 0:
                    e.wait_ge(s_t.h, s_t.n)
                cols = slice(gh0 * 64, (gh0 + HPG) * 64)
                wsem.inc(e.dma_start(out=wq3[:, :, :],
                            in_=wq_in.rearrange("(k p) c -> p k c", p=128)[:, :, cols]), 16)
                wsem.inc(e.dma_start(out=wk3[:, :, :],
                            in_=wk_in.rearrange("(k p) c -> p k c", p=128)[:, :, cols]), 16)
                wsem.inc(e.dma_start(out=wv3[:, :, :],
                                     in_=wv_in.rearrange("(k p) c -> p k c", p=128)[:, :, cols]), 16)
            w_ready = wsem.n

            pj_sem = s_sc if mode == "l1" else s_v
            prev_rope_ack = [0]
            pi = 0
            for hp in range(HPG // 2):
                for c in range(NCH):
                    for which in ("q", "k"):
                        pq = [ps0, ps1][pi % 2]
                        pqk = "AB"[pi % 2]
                        w3 = wq3 if which == "q" else wk3
                        dstf = qf3 if which == "q" else kf3

                        @block.tensor
                        def _(e, hp=hp, c=c, pq=pq, pqk=pqk, pi=pi, w3=w3, wsem=wsem, wr=w_ready, grp=grp):
                            if pi == 0:
                                e.wait_ge(s_sc.h, xn_ready)
                                e.wait_ge(wsem.h, wr)
                                if grp > 0:
                                    e.wait_ge(s_v.h, s_v.n)
                                    e.wait_ge(s_g.h, s_g.n)
                            use_bank(e, pqk)
                            for dt in range(8):
                                ins = e.matmul(pq[:, :], w3[:, dt, hp * 128:(hp + 1) * 128],
                                               xnf[:, dt, c * 512:(c + 1) * 512],
                                               start=(dt == 0), stop=(dt == 7))
                            s_t.inc(ins, 1)
                        mm_ack = s_t.n

                        if mode == "l1":
                            @block.scalar
                            def _(e, hp=hp, c=c, pq=pq, dstf=dstf, mm_ack=mm_ack):
                                e.wait_ge(s_t.h, mm_ack)
                                ins = e.copy(dstf[:, hp, c * 512:(c + 1) * 512], pq[:, :])
                                s_sc.inc(ins, 1)
                            set_bank(pqk, s_sc, s_sc.n)
                        else:
                            @block.scalar
                            def _(e, pq=pq, mm_ack=mm_ack, pra=prev_rope_ack[0]):
                                e.wait_ge(s_t.h, mm_ack)
                                if pra:
                                    e.wait_ge(s_v.h, pra)
                                e.copy(tmpf[:, :], pq[:, :])
                                s_sc.inc(e.copy(p_sb[:, 0:512], pq[:, :]), 1)
                            cp_ack = s_sc.n
                            set_bank(pqk, s_sc, cp_ack)
                            psw = [ps2, ps3][pi % 2]
                            pswk = "CD"[pi % 2]

                            @block.tensor
                            def _(e, psw=psw, pswk=pswk, cp_ack=cp_ack):
                                e.wait_ge(s_sc.h, cp_ack)
                                use_bank(e, pswk)
                                ins = e.matmul(psw[:, :], pswap_sb[:, :], p_sb[:, 0:512],
                                               start=True, stop=True)
                                s_t.inc(ins, 1)
                            sw_ack = s_t.n

                            @block.vector
                            def _(e, hp=hp, c=c, psw=psw, dstf=dstf, sw_ack=sw_ack, cp_ack=cp_ack):
                                e.wait_ge(s_sc.h, cp_ack)
                                e.wait_ge(s_t.h, sw_ack)
                                cc = ropec_sb[:, c * 512:(c + 1) * 512]
                                ss = ropes_sb[:, c * 512:(c + 1) * 512]
                                s_v.inc(e.tensor_tensor(tmpg[:, :], tmpf[:, :], cc, ALU.mult), 1)
                                e.wait_ge(s_v.h, s_v.n)
                                s_v.inc(e.tensor_tensor(tmpf[:, :], psw[:, :], ss, ALU.mult), 1)
                                e.wait_ge(s_v.h, s_v.n)
                                ins = e.tensor_tensor(dstf[:, hp, c * 512:(c + 1) * 512],
                                                      tmpg[:, :], tmpf[:, :], ALU.add)
                                s_v.inc(ins, 1)
                            set_bank(pswk, s_v, s_v.n)
                            prev_rope_ack[0] = s_v.n
                        pi += 1
            pj_done_s = (pj_sem, pj_sem.n)

            for i in range(NBLK):
                pvb = [ps2, ps3][i % 2]
                pvk = "CD"[i % 2]

                @block.tensor
                def _(e, i=i, pvb=pvb, pvk=pvk):
                    if i == 0:
                        e.wait_ge(pj_done_s[0].h, pj_done_s[1])
                    use_bank(e, pvk)
                    for dt in range(8):
                        ins = e.matmul(pvb[:, 0:HPG * 64], xnf[:, dt, i * 128:(i + 1) * 128],
                                       wv3[:, dt, :], start=(dt == 0), stop=(dt == 7))
                    s_t.inc(ins, 1)
                vmm_ack = s_t.n

                @block.vector
                def _(e, i=i, pvb=pvb, vmm_ack=vmm_ack):
                    e.wait_ge(s_t.h, vmm_ack)
                    if i == 0:
                        e.wait_ge(s_v.h, 1)
                    ins = e.tensor_copy(va4[:, :, i, 0:64],
                                        pvb[:, 0:HPG * 64].rearrange("p (h f) -> p h f", f=64))
                    s_v.inc(ins, 1)
                set_bank(pvk, s_v, s_v.n)

            for hl in range(HPG):
                hp, subb = hl // 2, (hl % 2) * 64

                @block.vector
                def _(e, hp=hp, subb=subb):
                    e.wait_ge(s_v.h, s_v.n)
                    e.wait_ge(s_t.h, s_t.n)
                    a = kf3[:, hp, :]
                    e.tensor_copy(k_aug[0:64, :], a[subb:subb + 64, :])
                    a = qf3[:, hp, :]
                    ins = e.tensor_copy(q_aug[0:64, :], a[subb:subb + 64, :])
                    s_v.inc(ins, 1)
                ka_ack = s_v.n

                for qb in range(NBLK):
                    nchunks = qb // 4 + 1
                    for kc in range(nchunks):
                        pqm = [ps0, ps1][kc % 2]
                        pqmk = "AB"[kc % 2]

                        @block.tensor
                        def _(e, qb=qb, kc=kc, hp=hp, subb=subb, pqm=pqm, pqmk=pqmk, ka=ka_ack):
                            if qb == 0 and kc == 0:
                                e.wait_ge(s_v.h, ka)
                            use_bank(e, pqmk)
                            qh = qf3[:, hp, qb * 128:(qb + 1) * 128][subb:subb + 64, :]
                            kh = kf3[:, hp, kc * 512:(kc + 1) * 512][subb:subb + 64, :]
                            ins = e.matmul(pqm[:, :], qh, kh, start=True, stop=True)
                            s_t.inc(ins, 1)
                        mma = s_t.n

                        @block.vector
                        def _(e, qb=qb, kc=kc, pqm=pqm, mma=mma):
                            e.wait_ge(s_t.h, mma)
                            if kc == qb // 4:
                                s_v.inc(e.tensor_tensor(tmpf[:, :], pqm[:, :],
                                                        mq3[:, qb % 4, :], ALU.add), 1)
                                e.wait_ge(s_v.h, s_v.n)
                                srcap = tmpf[:, :]
                            else:
                                srcap = pqm[:, :]
                            ins = e.reduce_max(m_all[:, kc:kc + 1], srcap,
                                               axis=AX.X, negate=True)
                            s_v.inc(ins, 1)
                        set_bank(pqmk, s_v, s_v.n)

                    @block.vector
                    def _(e, qb=qb, nch=nchunks):
                        e.wait_ge(s_v.h, s_v.n)
                        s_v.inc(e.tensor_reduce(m_all[:, 36:37], m_all[:, 0:nch],
                                                axis=AX.X, op=ALU.min), 1)
                        e.wait_ge(s_v.h, s_v.n)
                        ins = e.tensor_copy(m_bf[:, qb:qb + 1], m_all[:, 36:37])
                        s_v.inc(ins, 1)
                mb_ack = s_v.n

                @block.tensor
                def _(e, mb=mb_ack):
                    e.wait_ge(s_v.h, mb)
                    use_bank(e, "TM")
                    ins = e.transpose(ps_tm[0:NBLK, 0:128], m_bf[:, 0:NBLK], eye_sb[:, :])
                    s_t.inc(ins, 1)
                tmm_ack = s_t.n

                @block.scalar
                def _(e, tmm=tmm_ack):
                    e.wait_ge(s_t.h, tmm)
                    s_sc.inc(e.copy(m_row[0:NBLK, :], ps_tm[0:NBLK, 0:128]), 1)
                set_bank("TM", s_sc, s_sc.n)
                mrow_ack = s_sc.n

                @block.gpsimd
                def _(e, mrow=mrow_ack):
                    e.wait_ge(s_sc.h, mrow)
                    s_g.inc(e.dma_start(out=q_aug[64:65, :].rearrange("a (b t) -> a b t", b=NBLK),
                                        in_=m_row[0:NBLK, :]), 16)
                qa_ack = s_g.n

                norm_v_ack = [None]
                norm_s_ack = [None]
                last_diag_exp = [None]
                for qc in range(NCH):
                    ntiles = 4 * qc + 4
                    for j in range(ntiles):
                        plg = [ps0, ps1][j % 2]
                        plgk = "AB"[j % 2]
                        diag = j >= 4 * qc

                        @block.tensor
                        def _(e, qc=qc, j=j, plg=plg, plgk=plgk, qa=qa_ack):
                            if qc == 0 and j == 0:
                                e.wait_ge(s_g.h, qa)
                            use_bank(e, plgk)
                            ins = e.matmul(plg[:, :], k_aug[:, j * 128:(j + 1) * 128],
                                           q_aug[:, qc * 512:(qc + 1) * 512],
                                           start=True, stop=True, skip_group_check=True)
                            s_t.inc(ins, 1)
                        lg_ack = s_t.n

                        dm_ack = None
                        if diag:
                            @block.vector
                            def _(e, qc=qc, j=j, plg=plg, lg=lg_ack,
                                  pde=last_diag_exp[0]):
                                e.wait_ge(s_t.h, lg)
                                if pde is not None:
                                    e.wait_ge(s_sc.h, pde)
                                ins = e.tensor_tensor(tmpf[:, :], plg[:, :],
                                                      mk3[:, j - 4 * qc, :], ALU.add)
                                s_v.inc(ins, 1)
                            dm_ack = s_v.n

                        @block.scalar
                        def _(e, plg=plg, j=j, lg=lg_ack, dm=dm_ack, nva=norm_v_ack[0]):
                            e.wait_ge(s_t.h, lg)
                            if dm is not None:
                                e.wait_ge(s_v.h, dm)
                                srcap = tmpf[:, :]
                            else:
                                srcap = plg[:, :]
                            ins = e.activation(p_sb[:, (j % 2) * 512:(j % 2 + 1) * 512],
                                               srcap, AF.Exp, scale=scale)
                            s_sc.inc(ins, 1)
                        exp_ack = s_sc.n
                        if diag:
                            last_diag_exp[0] = exp_ack
                        if diag:
                            set_bank(plgk, s_v, dm_ack)
                        else:
                            set_bank(plgk, s_sc, exp_ack)

                        @block.tensor
                        def _(e, hl=hl, j=j, qc=qc, ntiles=ntiles, exp_ack=exp_ack,
                              nva=norm_v_ack[0], nsa=norm_s_ack[0]):
                            e.wait_ge(s_sc.h, exp_ack)
                            if j == 0:
                                if nva is not None:
                                    e.wait_ge(s_v.h, nva)
                                if nsa is not None:
                                    e.wait_ge(s_sc.h, nsa)
                            ins = e.matmul(ps_o[:, :], va4[:, hl, j, :],
                                           p_sb[:, (j % 2) * 512:(j % 2 + 1) * 512],
                                           start=(j == 0), stop=(j == ntiles - 1),
                                           skip_group_check=True)
                            s_t.inc(ins, 1)
                        if j >= 1:
                            pass
                    pv_last = s_t.n

                    @block.vector
                    def _(e, pv=pv_last):
                        e.wait_ge(s_t.h, pv)
                        s_v.inc(e.reciprocal(rden[:, :], ps_o[64:65, :]), 1)
                    rd_ack = s_v.n

                    @block.scalar
                    def _(e, pv=pv_last, nva=norm_v_ack[0]):
                        e.wait_ge(s_t.h, pv)
                        if nva is not None:
                            e.wait_ge(s_v.h, nva)
                        s_sc.inc(e.copy(onum[:, :], ps_o[0:64, :]), 1)
                    on_ack = s_sc.n

                    @block.tensor
                    def _(e, rd=rd_ack):
                        e.wait_ge(s_v.h, rd)
                        use_bank(e, "BC")
                        ins = e.matmul(ps_bc[:, :], ones64[:, :], rden[:, :],
                                       start=True, stop=True)
                        s_t.inc(ins, 1)
                    bc_ack = s_t.n

                    @block.vector
                    def _(e, hp=hp, subb=subb, qc=qc, bc=bc_ack, on=on_ack,
                          hl=hl, pos=prev_ostore[0]):
                        e.wait_ge(s_t.h, bc)
                        e.wait_ge(s_sc.h, on)
                        if hl == 0 and qc == 0 and pos:
                            e.wait_ge(s_st.h, pos)
                        dst = of3[:, hp, qc * 512:(qc + 1) * 512]
                        ins = e.tensor_tensor(dst[subb:subb + 64, :], onum[:, :],
                                              ps_bc[:, :], ALU.mult)
                        s_v.inc(ins, 1)
                    set_bank("BC", s_v, s_v.n)
                    norm_v_ack[0] = s_v.n
                    norm_s_ack[0] = s_sc.n

            @block.gpsimd
            def _(e, grp=grp):
                e.wait_ge(s_v.h, s_v.n)
                for hp in range(HPG // 2):
                    s_st.inc(e.dma_start(
                        out=o_out[(grp * (HPG // 2) + hp) * 128:(grp * (HPG // 2) + hp) * 128 + 128, :],
                        in_=of3[:, hp, :]), 16)
            prev_ostore[0] = s_st.n

        @block.gpsimd
        def _(e):
            e.wait_ge(s_st.h, s_st.n)
    return nc


def causal_masks():
    r = np.arange(128)[:, None]
    c = np.arange(512)[None, :]
    maskq = np.stack([np.where(c <= r + 128 * p, 0.0, -1e30) for p in range(4)]).astype(np.float32)
    maskk = np.stack([np.where(r + 128 * p <= c, 0.0, -1e30) for p in range(4)]).astype(np.float32)
    return maskq, maskk


def rope_consts(S):
    d = 64
    inv = 1.0 / (10000.0 ** (np.arange(0, d, 2) / d))
    ang = np.arange(S)[:, None] * inv[None, :]
    cos, sin = np.cos(ang).T, np.sin(ang).T
    cosF = np.concatenate([cos, cos, cos, cos], axis=0).astype(np.float32)
    sinF = np.concatenate([-sin, sin, -sin, sin], axis=0).astype(np.float32)
    perm = np.concatenate([np.arange(0, 64, 2), np.arange(1, 64, 2)])
    sw = np.zeros((128, 128), dtype=BFNP)
    for i in range(128):
        j = i + 32 if (i % 64) < 32 else i - 32
        sw[j, i] = 1.0
    return cosF, sinF, perm, sw


def run_causal_launch(x_full, Wq, Wk, Wv, mode="l1", g_ln=None):
    S = x_full.shape[1]
    eye = _eye_bf()
    maskq, maskk = causal_masks()
    if mode == "l2":
        cosF, sinF, perm, sw = rope_consts(S)
        permf = np.concatenate([h * 64 + perm for h in range(16)])
        Wq = Wq[:, permf]
        Wk = Wk[:, permf]
    nc = build_causal_attn(S=S, NHEAD=8, mode=mode)
    in_maps = []
    for c in range(NCORE):
        b, hh = c // 2, c % 2
        cols = slice(hh * 512, (hh + 1) * 512)
        im = {"x": np.ascontiguousarray(x_full[b]),
              "wq": np.ascontiguousarray(Wq[:, cols]),
              "wk": np.ascontiguousarray(Wk[:, cols]),
              "wv": np.ascontiguousarray(Wv[:, cols]),
              "maskq": maskq, "maskk": maskk, "eye": eye}
        if mode == "l1":
            im["gbc"] = np.ascontiguousarray(np.broadcast_to(g_ln, (128, D)).copy())
        else:
            im["ropec"] = cosF
            im["ropes"] = sinF
            im["pswap"] = sw
        in_maps.append(im)
    res = run_bass_kernel_spmd(nc, in_maps, core_ids=list(range(NCORE)))
    return [np.concatenate([res.results[2 * b]["o"], res.results[2 * b + 1]["o"]],
                           axis=0) for b in range(4)]



def build_swiglu(T=2048, F=3072):
    nc = bass.Bass("TRN2", target_bir_lowering=False)
    NB, NCH, NF = T // 128, T // 512, F // 128
    QS = 1.0 / 127.0
    x_in = nc.dram_tensor("x", [T, D], F32, kind="ExternalInput").ap()
    wg_in = nc.dram_tensor("wgT", [D, F], I32, kind="ExternalInput").ap()
    wu_in = nc.dram_tensor("wuT", [D, F], I32, kind="ExternalInput").ap()
    wd_in = nc.dram_tensor("wdT", [F, D], I32, kind="ExternalInput").ap()
    eye_in = nc.dram_tensor("eye", [128, 128], BF16, kind="ExternalInput").ap()
    y_out = nc.dram_tensor("y", [T, D], F32, kind="ExternalOutput").ap()

    x_src = x_in.rearrange("(t p) d -> p t d", p=128)
    y_dst = y_out.rearrange("(t p) d -> p t d", p=128)
    wg_src = wg_in.rearrange("(k p) f -> p k f", p=128)
    wu_src = wu_in.rearrange("(k p) f -> p k f", p=128)
    wd_src = wd_in.rearrange("(k p) d -> p k d", p=128)

    with ExitStack() as ctx:
        sb = lambda name, shape, dt: ctx.enter_context(nc.sbuf_tensor(name, shape, dt))
        ps = lambda name, shape, dt: ctx.enter_context(nc.psum_tensor(name, shape, dt))
        sem = lambda name: Sem(ctx.enter_context(nc.semaphore(name)))
        x_sb = sb("x_sb", [128, NB * D], F32)
        xn_bf = sb("xn_bf", [128, 4 * D], BF16)
        xn_fm = sb("xn_fm", [128, 8 * T], BF16)
        h_bf = sb("h_bf", [128, NF * 512], BF16)
        sg_bf = sb("sg_bf", [128, 512], BF16)
        w1_buf = sb("w1_buf", [128, 2 * 2 * 8 * 128], BF16)
        wd_bf = sb("wd_bf", [128, NF * D], BF16)
        eye_sb = sb("eye_sb", [128, 128], BF16)
        stats = sb("stats", [128, 64], F32)
        ps_h0 = ps("ps_h0", [128, 512], F32)
        ps_h1 = ps("ps_h1", [128, 512], F32)
        ps_y0 = ps("ps_y0", [128, 512], F32)
        ps_y1 = ps("ps_y1", [128, 512], F32)
        ps_t0 = ps("ps_t0", [128, 512], BF16)
        ps_t1 = ps("ps_t1", [128, 512], BF16)
        s_x, s_w, s_v = sem("s_x"), sem("s_w"), sem("s_v")
        s_sc, s_t, s_st = sem("s_sc"), sem("s_t"), sem("s_st")
        s_wa, s_wb = sem("s_wa"), sem("s_wb")
        s_w1 = [s_wa, s_wb]
        block = ctx.enter_context(nc.Block())
        ps_h, ps_y, ps_t = [ps_h0, ps_h1], [ps_y0, ps_y1], [ps_t0, ps_t1]
        x3 = x_sb[:].rearrange("p (t d) -> p t d", d=D)
        w1v = w1_buf[:].rearrange("p (b g k f) -> p b g k f", b=2, g=2, k=8)

        @block.sync
        def _(e):
            for c in range(NCH):
                s_x.inc(e.dma_start(out=x3[:, 4 * c:4 * (c + 1), :],
                                    in_=x_src[:, 4 * c:4 * (c + 1), :]), 16)
            s_x.inc(e.dma_start(out=eye_sb[:, :], in_=eye_in[:, :]), 16)

        @block.gpsimd
        def _(e):
            wdv = wd_bf[:].rearrange("p (k d) -> p k d", d=D)
            q = NF // 4
            for i in range(4):
                s_w.inc(e.dma_start(out=wdv[:, q * i:q * (i + 1), :],
                                    in_=wd_src[:, q * i:q * (i + 1), :]), 16)
        wd_ready = s_w.n

        w1_sent = []
        h_group_acks = []
        h_act_acks = []
        hm_acks = []
        y_evac_acks = []

        @block.gpsimd
        def _(e):
            for i in range(2):
                e.dma_start(out=w1v[:, i % 2, 0, :, :],
                            in_=wg_src[:, :, i * 128:(i + 1) * 128]).then_inc(s_w1[i % 2].h, 16)
                s_w1[i % 2].n += 16
                s_w1[i % 2].inc(e.dma_start(out=w1v[:, i % 2, 1, :, :],
                                            in_=wu_src[:, :, i * 128:(i + 1) * 128]), 16)
                w1_sent.append(s_w1[i % 2].n)

        for c in range(NCH):
            def pre(e, c=c):
                e.wait_ge(s_x.h, s_x.n)
                if c > 0:
                    e.wait_ge(s_t.h, s_t.n)
            emit_ln_chunk(block, nc,
                          lambda b, c=c: x3[:, 4 * c + b, :],
                          lambda b: xn_bf[:, b * D:(b + 1) * D],
                          stats, s_v, s_sc, nblk=4, pre_wait=pre)
            ln_ack = s_v.n

            @block.tensor
            def _(e, c=c, la=ln_ack):
                e.wait_ge(s_v.h, la)
                if c == 0:
                    e.wait_ge(s_x.h, s_x.n)
            emit_transpose_chunk(
                block, nc,
                lambda g, j, c=c: xn_bf[:, j * D + g * 128: j * D + g * 128 + 128],
                lambda g, c=c: xn_fm[:].rearrange("p (k t) -> p k t", t=T)[:, g, c * 512:(c + 1) * 512],
                eye_sb, ps_t, s_t, s_sc, ngroups=8)
            tp_done = s_sc.n

            for ft in range(NF):
                gi = c * NF + ft
                bank_g, bank_u = ps_h[gi % 2], ps_y[gi % 2]

                @block.tensor
                def _(e, ft=ft, gi=gi, c=c, bank_g=bank_g, bank_u=bank_u, tp=tp_done, sv_now=None):
                    e.wait_ge(s_sc.h, tp)
                    e.wait_ge(s_w1[gi % 2].h, w1_sent[gi])
                    if ft == 0:
                        e.wait_ge(s_v.h, s_v.n)
                    if gi >= 2:
                        e.wait_ge(s_v.h, hm_acks[gi - 2])
                    xv = xn_fm[:].rearrange("p (k t) -> p k t", t=T)
                    for dt in range(8):
                        e.matmul(bank_g[:, :], w1v[:, gi % 2, 0, dt, :],
                                 xv[:, dt, c * 512:(c + 1) * 512],
                                 start=(dt == 0), stop=(dt == 7))
                    for dt in range(8):
                        ins = e.matmul(bank_u[:, :], w1v[:, gi % 2, 1, dt, :],
                                       xv[:, dt, c * 512:(c + 1) * 512],
                                       start=(dt == 0), stop=(dt == 7))
                    h_group_acks.append(s_t.inc(ins, 1))

                @block.gpsimd
                def _(e, gi=gi):
                    nxt = gi + 2
                    if nxt < NCH * NF:
                        cc, fft = divmod(nxt, NF)
                        e.wait_ge(s_t.h, h_group_acks[gi])
                        e.dma_start(out=w1v[:, nxt % 2, 0, :, :],
                                    in_=wg_src[:, :, fft * 128:(fft + 1) * 128]).then_inc(s_w1[nxt % 2].h, 16)
                        s_w1[nxt % 2].n += 16
                        s_w1[nxt % 2].inc(e.dma_start(out=w1v[:, nxt % 2, 1, :, :],
                                                      in_=wu_src[:, :, fft * 128:(fft + 1) * 128]), 16)
                        w1_sent.append(s_w1[nxt % 2].n)

                @block.scalar
                def _(e, ft=ft, gi=gi, bank_g=bank_g):
                    e.wait_ge(s_t.h, h_group_acks[gi])
                    if gi >= 1:
                        e.wait_ge(s_v.h, hm_acks[gi - 1])
                    ins = e.activation(sg_bf[:, :], bank_g[:, :], AF.Silu, scale=QS)
                    h_act_acks.append(s_sc.inc(ins, 1))

                @block.vector
                def _(e, ft=ft, gi=gi, bank_u=bank_u):
                    e.wait_ge(s_sc.h, h_act_acks[gi])
                    ins = e.scalar_tensor_tensor(h_bf[:, ft * 512:(ft + 1) * 512],
                                                 bank_u[:, :], QS, sg_bf[:, :],
                                                 ALU.mult, ALU.mult)
                    hm_acks.append(s_v.inc(ins, 1))
            h_done = s_v.n

            y_last_ack = [None]
            for blk in range(4):
                for dc in range(2):
                    gj = (c * 4 + blk) * 2 + dc
                    bank = ps_y[gj % 2] if False else [ps_h0, ps_h1][gj % 2]

                    @block.tensor
                    def _(e, blk=blk, dc=dc, gj=gj, c=c, bank=bank, hd=h_done):
                        e.wait_ge(s_v.h, hd)
                        if gj == 0:
                            e.wait_ge(s_w.h, wd_ready)
                        if gj >= 2:
                            e.wait_ge(s_v.h, y_evac_acks[gj - 2])
                        h3 = h_bf[:].rearrange("p (k t) -> p k t", t=512)
                        wd3 = wd_bf[:].rearrange("p (k d) -> p k d", d=D)
                        for ft in range(NF):
                            ins = e.matmul(bank[:, :], h3[:, ft, blk * 128:(blk + 1) * 128],
                                           wd3[:, ft, dc * 512:(dc + 1) * 512],
                                           start=(ft == 0), stop=(ft == NF - 1))
                        y_last_ack[0] = s_t.inc(ins, 1)

                    @block.vector
                    def _(e, blk=blk, dc=dc, bank=bank, c=c):
                        e.wait_ge(s_t.h, s_t.n)
                        dst = x3[:, 4 * c + blk, dc * 512:(dc + 1) * 512]
                        ins = e.scalar_tensor_tensor(dst, bank[:, :], 0.5 * QS,
                                                     dst, ALU.mult, ALU.add)
                        y_evac_acks.append(s_v.inc(ins, 1))

            @block.sync
            def _(e, c=c):
                e.wait_ge(s_v.h, s_v.n)
                s_st.inc(e.dma_start(out=y_dst[:, 4 * c:4 * (c + 1), :],
                                     in_=x3[:, 4 * c:4 * (c + 1), :]), 16)

        @block.sync
        def _(e):
            e.wait_ge(s_st.h, s_st.n)
    return nc


def run_swiglu_launch(x_full, w_gate, w_up, w_down):
    T = x_full.shape[0] // NCORE
    nc = build_swiglu(T=T, F=w_gate.shape[0])
    wgT = np.ascontiguousarray(w_gate.T)
    wuT = np.ascontiguousarray(w_up.T)
    wdT = np.ascontiguousarray(w_down.T)
    eye = _eye_bf()
    in_maps = [{"x": np.ascontiguousarray(x_full[c * T:(c + 1) * T]),
                "wgT": wgT, "wuT": wuT, "wdT": wdT, "eye": eye}
               for c in range(NCORE)]
    res = run_bass_kernel_spmd(nc, in_maps, core_ids=list(range(NCORE)))
    return np.concatenate([res.results[c]["y"] for c in range(NCORE)], axis=0)



def vch(e, s, ins):
    s.inc(ins, 1)
    e.wait_ge(s.h, s.n)
    return ins


def build_mod(T=2048, F=4096, NITER=44):
    nc = bass.Bass("TRN2", target_bir_lowering=False)
    NB, NCH, NF = T // 128, T // 512, F // 128
    x_in = nc.dram_tensor("x", [T, D], F32, kind="ExternalInput").ap()
    xo_in = nc.dram_tensor("xo", [T, D], F32, kind="ExternalInput").ap()
    w1_in = nc.dram_tensor("w1", [D, F], F32, kind="ExternalInput").ap()
    b1_in = nc.dram_tensor("b1c", [128, NF], F32, kind="ExternalInput").ap()
    w2_in = nc.dram_tensor("w2", [F, D], F32, kind="ExternalInput").ap()
    b2_in = nc.dram_tensor("b2r", [1, D], F32, kind="ExternalInput").ap()
    wr_in = nc.dram_tensor("wrbc", [128, D], F32, kind="ExternalInput").ap()
    eye_in = nc.dram_tensor("eye", [128, 128], BF16, kind="ExternalInput").ap()
    y_out = nc.dram_tensor("y", [T, D], F32, kind="ExternalOutput").ap()

    x_src = x_in.rearrange("(t p) d -> p t d", p=128)
    xo_src = xo_in.rearrange("(t p) d -> p t d", p=128)
    y_dst = y_out.rearrange("(t p) d -> p t d", p=128)
    w1_src = w1_in.rearrange("(k p) f -> p k f", p=128)
    w2_src = w2_in.rearrange("(k p) d -> p k d", p=128)

    with ExitStack() as ctx:
        sb = lambda name, shape, dt: ctx.enter_context(nc.sbuf_tensor(name, shape, dt))
        ps = lambda name, shape, dt: ctx.enter_context(nc.psum_tensor(name, shape, dt))
        sem = lambda name: Sem(ctx.enter_context(nc.semaphore(name)))
        x_ch = sb("x_ch", [128, 4 * D], F32)
        xn_f = sb("xn_f", [128, D], F32)
        xn_bf = sb("xn_bf", [128, 4 * D], BF16)
        xn_fm = sb("xn_fm", [128, 8 * T], BF16)
        h_bf = sb("h_bf", [128, NF * 512], BF16)
        w1_buf = sb("w1_buf", [128, 2 * 8 * 128], BF16)
        w2_bf = sb("w2_bf", [128, NF * D], BF16)
        b1_sb = sb("b1_sb", [128, NF], F32)
        b2_sb = sb("b2_sb", [1, D], BF16)
        wr_sb = sb("wr_sb", [128, D], F32)
        tmps = sb("tmps", [128, D], F32)
        scores = sb("scores", [128, 2 * NB], F32)
        bis = sb("bis", [128, 8], F32)
        ones128c = sb("ones128c", [128, 1], F32)
        ones1f = sb("ones1f", [1, 128], F32)
        ws2 = sb("ws2", [128, NB], F32)
        ones1 = sb("ones1", [1, 128], BF16)
        eye_sb = sb("eye_sb", [128, 128], BF16)
        ps_h0 = ps("ps_h0", [128, 512], F32)
        ps_h1 = ps("ps_h1", [128, 512], F32)
        ps_y0 = ps("ps_y0", [128, 512], F32)
        ps_y1 = ps("ps_y1", [128, 512], F32)
        ps_t0 = ps("ps_t0", [128, 512], BF16)
        ps_t1 = ps("ps_t1", [128, 512], BF16)
        ps_c = ps("ps_c", [128, 128], F32)
        stats = ps_c[:, 4:68]
        s_x, s_w, s_v = sem("s_x"), sem("s_w"), sem("s_v")
        s_sc, s_t, s_st = sem("s_sc"), sem("s_t"), sem("s_st")
        s_wa, s_wb = sem("s_wa"), sem("s_wb")
        s_w1 = [s_wa, s_wb]
        block = ctx.enter_context(nc.Block())
        ps_h, ps_y, ps_t = [ps_h0, ps_h1], [ps_y0, ps_y1], [ps_t0, ps_t1]
        x3 = x_ch[:].rearrange("p (t d) -> p t d", d=D)
        xnf = xn_fm[:].rearrange("p (k t) -> p k t", t=T)
        w1v = w1_buf[:].rearrange("p (b k f) -> p b k f", b=2, k=8)

        @block.sync
        def _(e):
            s_x.inc(e.dma_start(out=x3[:, :, :], in_=x_src[:, 0:4, :]), 16)
            s_x.inc(e.dma_start(out=b1_sb[:, :], in_=b1_in[:, :]), 16)
            s_x.inc(e.dma_start(out=eye_sb[:, :], in_=eye_in[:, :]), 16)
            s_x.inc(e.dma_start(out=wr_sb[:, :], in_=wr_in[:, :]), 16)

        @block.gpsimd
        def _(e):
            s_w.inc(e.dma_start(out=b2_sb[:, :], in_=b2_in[:, :]), 16)
            w2v = w2_bf[:].rearrange("p (k d) -> p k d", d=D)
            q = NF // 4
            for i in range(4):
                s_w.inc(e.dma_start(out=w2v[:, q * i:q * (i + 1), :],
                                    in_=w2_src[:, q * i:q * (i + 1), :]), 16)
        w2_ready = s_w.n

        @block.vector
        def _(e):
            e.memset(ones1[:, :], 1.0)
            e.memset(ones1f[:, :], 1.0)
            s_v.inc(e.memset(ones128c[:, :], 1.0), 1)

        w1_sent = []
        h_group_acks = []
        h_act_acks = []
        y_evac_acks = []

        @block.gpsimd
        def _(e):
            for i in range(2):
                w1_sent.append(
                    s_w1[i % 2].inc(e.dma_start(out=w1v[:, i % 2, :, :],
                                                in_=w1_src[:, :, i * 128:(i + 1) * 128]), 16))

        for c in range(NCH):
            for b in range(4):
                def pre(e, c=c, b=b):
                    e.wait_ge(s_x.h, 16 * (c + 1) + 48)
                    e.wait_ge(s_v.h, s_v.n)
                    if c > 0 and b == 0:
                        e.wait_ge(s_t.h, s_t.n)
                emit_ln_chunk(block, nc,
                              lambda _b, c=c, b=b: x3[:, b, :],
                              lambda _b: xn_f[:, :],
                              stats, s_v, s_sc, nblk=1, pre_wait=pre)

                @block.vector
                def _(e, c=c, b=b):
                    e.wait_ge(s_v.h, s_v.n)
                    e.wait_ge(s_x.h, s_x.n)
                    e.tensor_copy(xn_bf[:, b * D:(b + 1) * D], xn_f[:, :])
                    vch(e, s_v, e.tensor_tensor(tmps[:, :], xn_f[:, :], wr_sb[:, :], ALU.mult))
                    ins = vch(e, s_v, e.reduce_sum(scores[:, 4 * c + b:4 * c + b + 1],
                                                   tmps[:, :], axis=AX.X))
            ln_ack = s_v.n

            @block.tensor
            def _(e, la=ln_ack):
                e.wait_ge(s_v.h, la)
            emit_transpose_chunk(
                block, nc,
                lambda g, j, c=c: xn_bf[:, j * D + g * 128: j * D + g * 128 + 128],
                lambda g, c=c: xnf[:, g, c * 512:(c + 1) * 512],
                eye_sb, ps_t, s_t, s_sc, ngroups=8)

            @block.sync
            def _(e, c=c, la=ln_ack):
                if c + 1 < NCH:
                    e.wait_ge(s_v.h, la)
                    s_x.inc(e.dma_start(out=x3[:, :, :],
                                        in_=x_src[:, 4 * (c + 1):4 * (c + 2), :]), 16)
        xn_ready = s_sc.n

        for blk in range(NB):
            @block.sync
            def _(e, blk=blk):
                e.wait_ge(s_v.h, s_v.n)
                s_x.inc(e.dma_start(out=xn_f[:, :], in_=xo_src[:, blk, :]), 16)

            def pre2(e):
                e.wait_ge(s_x.h, s_x.n)
            emit_ln_chunk(block, nc,
                          lambda _b: xn_f[:, :],
                          lambda _b: xn_f[:, :],
                          stats, s_v, s_sc, nblk=1, pre_wait=pre2)

            @block.vector
            def _(e, blk=blk):
                e.wait_ge(s_v.h, s_v.n)
                vch(e, s_v, e.tensor_tensor(tmps[:, :], xn_f[:, :], wr_sb[:, :], ALU.mult))
                vch(e, s_v, e.reduce_sum(scores[:, NB + blk:NB + 1 + blk],
                                         tmps[:, :], axis=AX.X))

        KSEL = T
        @block.vector
        def _(e):
            e.wait_ge(s_v.h, s_v.n)
            e.memset(bis[:, 0:1], -1e5)
            vch(e, s_v, e.memset(bis[:, 1:2], 1e5))
        cnt_col = 2 * NB
        prev_cp_ack = [0]
        for it in range(NITER):
            @block.vector
            def _(e, it=it):
                vch(e, s_v, e.scalar_tensor_tensor(bis[:, 2:3], bis[:, 0:1], 1.0,
                                                   bis[:, 1:2], ALU.mult, ALU.add))
                vch(e, s_v, e.tensor_scalar_mul(bis[:, 2:3], bis[:, 2:3], 0.5))
                vch(e, s_v, e.tensor_scalar(tmps[:, 0:2 * NB], scores[:, :], bis[:, 2:3],
                                            0.0, ALU.is_ge, ALU.add))
                ins = e.reduce_sum(tmps[:, cnt_col:cnt_col + 1], tmps[:, 0:2 * NB], axis=AX.X)
                s_v.inc(ins, 1)
            cnt_ack = s_v.n

            @block.tensor
            def _(e, ca=cnt_ack, pca=prev_cp_ack[0]):
                e.wait_ge(s_v.h, ca)
                if pca:
                    e.wait_ge(s_sc.h, pca)
                ins = e.matmul(ps_c[0:1, 0:1], tmps[:, cnt_col:cnt_col + 1], ones128c[:, :],
                               start=True, stop=True)
                s_t.inc(ins, 1)
            t1_ack = s_t.n

            @block.scalar
            def _(e, t1=t1_ack):
                e.wait_ge(s_t.h, t1)
                s_sc.inc(e.copy(bis[0:1, 5:6], ps_c[0:1, 0:1]), 1)
            cp_ack = s_sc.n
            prev_cp_ack[0] = cp_ack

            @block.tensor
            def _(e, cp=cp_ack):
                e.wait_ge(s_sc.h, cp)
                ins = e.matmul(ps_c[:, 1:2], ones1f[0:1, :], bis[0:1, 5:6],
                               start=True, stop=True)
                s_t.inc(ins, 1)
            t2_ack = s_t.n

            @block.vector
            def _(e, t2=t2_ack):
                e.wait_ge(s_t.h, t2)
                vch(e, s_v, e.tensor_scalar(bis[:, 3:4], ps_c[:, 1:2], float(KSEL),
                                            0.0, ALU.is_ge, ALU.add))
                vch(e, s_v, e.tensor_tensor(bis[:, 4:5], bis[:, 2:3], bis[:, 0:1], ALU.subtract))
                vch(e, s_v, e.tensor_tensor(bis[:, 4:5], bis[:, 4:5], bis[:, 3:4], ALU.mult))
                vch(e, s_v, e.tensor_tensor(bis[:, 0:1], bis[:, 0:1], bis[:, 4:5], ALU.add))
                vch(e, s_v, e.tensor_scalar(bis[:, 6:7], bis[:, 3:4], -1.0, 1.0, ALU.mult, ALU.add))
                vch(e, s_v, e.tensor_tensor(bis[:, 4:5], bis[:, 1:2], bis[:, 2:3], ALU.subtract))
                vch(e, s_v, e.tensor_tensor(bis[:, 4:5], bis[:, 4:5], bis[:, 6:7], ALU.mult))
                vch(e, s_v, e.tensor_tensor(bis[:, 1:2], bis[:, 1:2], bis[:, 4:5], ALU.subtract))

        @block.vector
        def _(e):
            vch(e, s_v, e.tensor_scalar(tmps[:, 0:NB], scores[:, 0:NB], bis[:, 0:1],
                                        0.5, ALU.is_ge, ALU.mult))
        sig_ack0 = s_v.n

        @block.scalar
        def _(e, sa=sig_ack0):
            e.wait_ge(s_v.h, sa)
            s_sc.inc(e.activation(tmps[:, NB:2 * NB], scores[:, 0:NB], AF.Sigmoid), 1)
        sig_ack = s_sc.n

        @block.vector
        def _(e, sa=sig_ack):
            e.wait_ge(s_sc.h, sa)
            vch(e, s_v, e.tensor_tensor(ws2[:, 0:NB], tmps[:, 0:NB], tmps[:, NB:2 * NB], ALU.mult))
        ws_ack = s_v.n

        for c in range(NCH):
            @block.sync
            def _(e, c=c):
                e.wait_ge(s_v.h, s_v.n)
                if c > 0:
                    e.wait_ge(s_st.h, s_st.n)
                s_x.inc(e.dma_start(out=x3[:, :, :],
                                    in_=x_src[:, 4 * c:4 * (c + 1), :]), 16)

            for b in range(4):
                def pre3(e, c=c, b=b):
                    e.wait_ge(s_x.h, s_x.n)
                    e.wait_ge(s_v.h, s_v.n)
                emit_ln_chunk(block, nc,
                              lambda _b, c=c, b=b: x3[:, b, :],
                              lambda _b: xn_f[:, :],
                              stats, s_v, s_sc, nblk=1, pre_wait=pre3)

                @block.vector
                def _(e, c=c, b=b):
                    e.wait_ge(s_v.h, s_v.n)
                    ins = e.scalar_tensor_tensor(x3[:, b, :], xn_f[:, :], 0.5,
                                                 x3[:, b, :], ALU.mult, ALU.add)
                    s_v.inc(ins, 1)
            xup_ack = s_v.n

            for ft in range(NF):
                gi = c * NF + ft
                bank = ps_h[gi % 2]

                @block.tensor
                def _(e, ft=ft, gi=gi, c=c, bank=bank):
                    if gi == 0:
                        e.wait_ge(s_sc.h, xn_ready)
                    e.wait_ge(s_w1[gi % 2].h, w1_sent[gi])
                    if gi >= 2:
                        e.wait_ge(s_sc.h, h_act_acks[gi - 2])
                    if ft == 0:
                        e.wait_ge(s_v.h, s_v.n)
                    for dt in range(8):
                        ins = e.matmul(bank[:, :], w1v[:, gi % 2, dt, :],
                                       xnf[:, dt, c * 512:(c + 1) * 512],
                                       start=(dt == 0), stop=(dt == 7))
                    h_group_acks.append(s_t.inc(ins, 1))

                @block.gpsimd
                def _(e, gi=gi):
                    nxt = gi + 2
                    if nxt < NCH * NF:
                        cc, fft = divmod(nxt, NF)
                        e.wait_ge(s_t.h, h_group_acks[gi])
                        w1_sent.append(
                            s_w1[nxt % 2].inc(e.dma_start(out=w1v[:, nxt % 2, :, :],
                                                          in_=w1_src[:, :, fft * 128:(fft + 1) * 128]), 16))

                @block.scalar
                def _(e, ft=ft, gi=gi, bank=bank):
                    e.wait_ge(s_t.h, h_group_acks[gi])
                    if gi == 0:
                        e.wait_ge(s_x.h, s_x.n)
                    ins = e.activation(h_bf[:, ft * 512:(ft + 1) * 512],
                                       bank[:, :], AF.Gelu_apprx_tanh,
                                       bias=b1_sb[:, ft:ft + 1])
                    h_act_acks.append(s_sc.inc(ins, 1))
            h_done = s_sc.n

            for blk in range(4):
                for dc in range(2):
                    gj = (c * 4 + blk) * 2 + dc
                    bank = ps_y[gj % 2]

                    @block.tensor
                    def _(e, blk=blk, dc=dc, gj=gj, c=c, bank=bank, hd=h_done):
                        e.wait_ge(s_sc.h, hd)
                        if gj == 0:
                            e.wait_ge(s_w.h, w2_ready)
                            e.wait_ge(s_v.h, s_v.n)
                        if gj >= 2:
                            e.wait_ge(s_v.h, y_evac_acks[gj - 2])
                        h3 = h_bf[:].rearrange("p (k t) -> p k t", t=512)
                        w23 = w2_bf[:].rearrange("p (k d) -> p k d", d=D)
                        for ft in range(NF):
                            e.matmul(bank[:, :], h3[:, ft, blk * 128:(blk + 1) * 128],
                                     w23[:, ft, dc * 512:(dc + 1) * 512],
                                     start=(ft == 0), stop=False)
                        ins = e.matmul(bank[:, :], ones1[:, :],
                                       b2_sb[:, dc * 512:(dc + 1) * 512],
                                       start=False, stop=True)
                        s_t.inc(ins, 1)

                    @block.vector
                    def _(e, blk=blk, dc=dc, gj=gj, bank=bank, c=c):
                        e.wait_ge(s_t.h, s_t.n)
                        dst = x3[:, blk, dc * 512:(dc + 1) * 512]
                        ins = e.scalar_tensor_tensor(dst, bank[:, :],
                                                     ws2[:, 4 * c + blk:4 * c + blk + 1],
                                                     dst, ALU.mult, ALU.add)
                        y_evac_acks.append(s_v.inc(ins, 1))

            @block.sync
            def _(e, c=c):
                e.wait_ge(s_v.h, s_v.n)
                s_st.inc(e.dma_start(out=y_dst[:, 4 * c:4 * (c + 1), :],
                                     in_=x3[:, :, :]), 16)

        @block.sync
        def _(e):
            e.wait_ge(s_st.h, s_st.n)
    return nc


def run_mod_launch(x_full, w_router, W1, b1, W2, b2):
    T = x_full.shape[0] // NCORE
    F = W1.shape[1]
    nc = build_mod(T=T, F=F)
    b1c = np.ascontiguousarray(b1.reshape(F // 128, 128).T)
    b2r = np.ascontiguousarray(b2.reshape(1, D))
    wrbc = np.ascontiguousarray(np.broadcast_to(w_router, (128, D)))
    eye = _eye_bf()
    in_maps = []
    for c in range(NCORE):
        peer = c ^ 1
        in_maps.append({
            "x": np.ascontiguousarray(x_full[c * T:(c + 1) * T]),
            "xo": np.ascontiguousarray(x_full[peer * T:(peer + 1) * T]),
            "w1": W1, "b1c": b1c, "w2": W2, "b2r": b2r, "wrbc": wrbc,
            "eye": eye})
    res = run_bass_kernel_spmd(nc, in_maps, core_ids=list(range(NCORE)))
    return np.concatenate([res.results[c]["y"] for c in range(NCORE)], axis=0)


def build_moe(T=2048, F=4096, NE=8):
    nc = bass.Bass("TRN2", target_bir_lowering=False)
    NB, NCH, NF = T // 128, T // 512, F // 128
    x_in = nc.dram_tensor("x", [T, D], F32, kind="ExternalInput").ap()
    wr_in = nc.dram_tensor("wr", [D, NE], F32, kind="ExternalInput").ap()
    w1_in = nc.dram_tensor("w1", [NE, D, F], F32, kind="ExternalInput").ap()
    b1_in = nc.dram_tensor("b1c", [128, NE * NF], F32, kind="ExternalInput").ap()
    w2_in = nc.dram_tensor("w2", [NE, F, D], F32, kind="ExternalInput").ap()
    b2_in = nc.dram_tensor("b2r", [NE, D], F32, kind="ExternalInput").ap()
    eye_in = nc.dram_tensor("eye", [128, 128], BF16, kind="ExternalInput").ap()
    y_out = nc.dram_tensor("y", [T, D], F32, kind="ExternalOutput").ap()

    x_src = x_in.rearrange("(t p) d -> p t d", p=128)
    y_dst = y_out.rearrange("(t p) d -> p t d", p=128)
    w1_src = w1_in.rearrange("e (k p) f -> e p k f", p=128)
    w2_src = w2_in.rearrange("e (k p) d -> e p k d", p=128)

    with ExitStack() as ctx:
        sb = lambda name, shape, dt: ctx.enter_context(nc.sbuf_tensor(name, shape, dt))
        ps = lambda name, shape, dt: ctx.enter_context(nc.psum_tensor(name, shape, dt))
        sem = lambda name: Sem(ctx.enter_context(nc.semaphore(name)))
        x_sb = sb("x_sb", [128, NB * D], F32)
        xn_bf = sb("xn_bf", [128, 4 * D], BF16)
        xn_fm = sb("xn_fm", [128, 8 * T], BF16)
        h_bf = sb("h_bf", [128, NF * 512], BF16)
        w1_buf = sb("w1_buf", [128, 2 * 8 * 128], BF16)
        w2_bf = sb("w2_bf", [128, NF * D], BF16)
        b1_sb = sb("b1_sb", [128, 2 * NF], F32)
        b2_sb = sb("b2_sb", [1, D], BF16)
        wr_sb = sb("wr_sb", [128, 8 * NE], F32)
        wr_hi = sb("wr_hi", [128, 8 * NE], BF16)
        wr_hf = sb("wr_hf", [128, 8 * NE], F32)
        wr_lo = sb("wr_lo", [128, 8 * NE], BF16)
        ones1 = sb("ones1", [1, 128], BF16)
        eye_sb = sb("eye_sb", [128, 128], BF16)
        ps_h0 = ps("ps_h0", [128, 512], F32)
        ps_h1 = ps("ps_h1", [128, 512], F32)
        ps_y0 = ps("ps_y0", [128, 512], F32)
        ps_y1 = ps("ps_y1", [128, 512], F32)
        ps_t0 = ps("ps_t0", [128, 512], BF16)
        ps_t1 = ps("ps_t1", [128, 512], BF16)
        ps_r = ps("ps_r", [128, 128], F32)
        stats = ps_r[:, 16:80]
        rtr = sb("rtr", [128, 48], F32)
        ps_g = ps("ps_g", [128, 16 * NE], F32)
        gts = ps_g
        s_x, s_w, s_v = sem("s_x"), sem("s_w"), sem("s_v")
        s_sc, s_t, s_st = sem("s_sc"), sem("s_t"), sem("s_st")
        s_wa, s_wb = sem("s_wa"), sem("s_wb")
        s_w2 = sem("s_w2")
        s_w1 = [s_wa, s_wb]
        block = ctx.enter_context(nc.Block())
        ps_h, ps_y, ps_t = [ps_h0, ps_h1], [ps_y0, ps_y1], [ps_t0, ps_t1]
        x3 = x_sb[:].rearrange("p (t d) -> p t d", d=D)
        xnf = xn_fm[:].rearrange("p (k t) -> p k t", t=T)
        w1v = w1_buf[:].rearrange("p (b k f) -> p b k f", b=2, k=8)
        wr3 = lambda t: t[:].rearrange("p (k e) -> p k e", e=NE)

        @block.sync
        def _(e):
            for c in range(NCH):
                s_x.inc(e.dma_start(out=x3[:, 4 * c:4 * (c + 1), :],
                                    in_=x_src[:, 4 * c:4 * (c + 1), :]), 16)
            s_x.inc(e.dma_start(out=b1_sb[:, 0:NF], in_=b1_in[:, 0:NF]), 16)
            s_x.inc(e.dma_start(out=eye_sb[:, :], in_=eye_in[:, :]), 16)
            s_x.inc(e.dma_start(out=wr3(wr_sb), in_=wr_in.rearrange("(k p) e -> p k e", p=128)), 16)

        @block.gpsimd
        def _(e):
            s_w.inc(e.dma_start(out=b2_sb[0:1, 0:D], in_=b2_in[0:1, :]), 16)
            w2v = w2_bf[:].rearrange("p (k d) -> p k d", d=D)
            q = NF // 4
            for i in range(4):
                s_w2.inc(e.dma_start(out=w2v[:, q * i:q * (i + 1), :],
                                     in_=w2_src[0, :, q * i:q * (i + 1), :]), 16)

        @block.vector
        def _(e):
            s_v.inc(e.memset(ones1[:, :], 1.0), 1)

        @block.vector
        def _(e):
            e.wait_ge(s_x.h, s_x.n)
            vch(e, s_v, e.tensor_copy(wr_hi[:, :], wr_sb[:, :]))
            vch(e, s_v, e.tensor_copy(wr_hf[:, :], wr_hi[:, :]))
            vch(e, s_v, e.tensor_tensor(wr_hf[:, :], wr_sb[:, :], wr_hf[:, :], ALU.subtract))
            vch(e, s_v, e.tensor_copy(wr_lo[:, :], wr_hf[:, :]))
        wrsplit_ack = s_v.n

        w1_sent = []
        h_group_acks = []
        h_act_acks = []
        y_evac_acks = []
        NTILE_TOT = NE * NCH * NF

        def w1src_of(gi):
            ei, r = divmod(gi, NCH * NF)
            cc, fft = divmod(r, NF)
            return ei, fft

        @block.gpsimd
        def _(e):
            for i in range(2):
                ei, fft = w1src_of(i)
                w1_sent.append(
                    s_w1[i % 2].inc(e.dma_start(out=w1v[:, i % 2, :, :],
                                                in_=w1_src[ei, :, :, fft * 128:(fft + 1) * 128]), 16))

        rtr_last = [0]
        tp_dones = []
        for c in range(NCH):
            def pre(e, c=c):
                e.wait_ge(s_x.h, s_x.n)
                if c > 0:
                    e.wait_ge(s_t.h, s_t.n)
            emit_ln_chunk(block, nc,
                          lambda b, c=c: x3[:, 4 * c + b, :],
                          lambda b: xn_bf[:, b * D:(b + 1) * D],
                          stats, s_v, s_sc, nblk=4, pre_wait=pre)
            ln_ack = s_v.n

            @block.tensor
            def _(e, la=ln_ack):
                e.wait_ge(s_v.h, la)
            emit_transpose_chunk(
                block, nc,
                lambda g, j, c=c: xn_bf[:, j * D + g * 128: j * D + g * 128 + 128],
                lambda g, c=c: xnf[:, g, c * 512:(c + 1) * 512],
                eye_sb, ps_t, s_t, s_sc, ngroups=8)
            tp_dones.append(s_sc.n)

            for b in range(4):
                blk = 4 * c + b

                @block.tensor
                def _(e, c=c, blk=blk, tp=tp_dones[-1], rl=rtr_last[0]):
                    e.wait_ge(s_sc.h, tp)
                    e.wait_ge(s_v.h, wrsplit_ack)
                    if rl:
                        e.wait_ge(s_v.h, rl)
                    for dt in range(8):
                        e.matmul(ps_r[:, 0:NE], xnf[:, dt, blk * 128:(blk + 1) * 128],
                                 wr3(wr_hi)[:, dt, :], start=(dt == 0), stop=False)
                    for dt in range(8):
                        ins = e.matmul(ps_r[:, 0:NE], xnf[:, dt, blk * 128:(blk + 1) * 128],
                                       wr3(wr_lo)[:, dt, :], start=False, stop=(dt == 7))
                    s_t.inc(ins, 1)
                rt_ack = s_t.n

                @block.vector
                def _(e, rt=rt_ack):
                    e.wait_ge(s_t.h, rt)
                    ins = e.reduce_max(rtr[:, 8:9], ps_r[:, 0:NE], axis=AX.X, negate=True)
                    s_v.inc(ins, 1)
                mx_ack = s_v.n

                @block.scalar
                def _(e, mx=mx_ack):
                    e.wait_ge(s_v.h, mx)
                    ins = e.activation(rtr[:, 16:16 + NE], ps_r[:, 0:NE], AF.Exp,
                                       bias=rtr[:, 8:9], accum_out=rtr[:, 9:10])
                    s_sc.inc(ins, 1)
                ex_ack = s_sc.n

                @block.vector
                def _(e, blk=blk, ex=ex_ack):
                    e.wait_ge(s_sc.h, ex)
                    vch(e, s_v, e.reciprocal(rtr[:, 10:11], rtr[:, 9:10]))
                    vch(e, s_v, e.tensor_scalar(rtr[:, 24:24 + NE], rtr[:, 16:16 + NE],
                                                rtr[:, 10:11], 0.0, ALU.mult, ALU.add))
                    vch(e, s_v, e.reduce_max(rtr[:, 11:12], rtr[:, 24:24 + NE],
                                             axis=AX.X, negate=True))
                    vch(e, s_v, e.tensor_scalar_mul(rtr[:, 12:13], rtr[:, 11:12], -1.0))
                    vch(e, s_v, e.tensor_scalar(rtr[:, 32:32 + NE], rtr[:, 24:24 + NE],
                                                rtr[:, 12:13], 0.0, ALU.is_ge, ALU.add))
                    vch(e, s_v, e.tensor_scalar(rtr[:, 40:40 + NE], rtr[:, 32:32 + NE],
                                                -1.0, 1.0, ALU.mult, ALU.add))
                    vch(e, s_v, e.tensor_tensor(rtr[:, 40:40 + NE], rtr[:, 24:24 + NE],
                                                rtr[:, 40:40 + NE], ALU.mult))
                    vch(e, s_v, e.reduce_max(rtr[:, 11:12], rtr[:, 40:40 + NE],
                                             axis=AX.X, negate=True))
                    ins = e.tensor_scalar_mul(rtr[:, 13:14], rtr[:, 11:12], -1.0)
                    s_v.inc(ins, 1)
                p2_ack = s_v.n

                @block.scalar
                def _(e, p2=p2_ack):
                    e.wait_ge(s_v.h, p2)
                    s_sc.inc(e.activation(rtr[:, 14:16], rtr[:, 12:14], AF.Exp), 1)
                e12_ack = s_sc.n

                @block.vector
                def _(e, blk=blk, e12=e12_ack):
                    e.wait_ge(s_sc.h, e12)
                    vch(e, s_v, e.tensor_scalar(rtr[:, 40:40 + NE], rtr[:, 40:40 + NE],
                                                rtr[:, 13:14], 0.0, ALU.is_ge, ALU.add))
                    vch(e, s_v, e.tensor_tensor(rtr[:, 9:10], rtr[:, 14:15],
                                                rtr[:, 15:16], ALU.add))
                    vch(e, s_v, e.reciprocal(rtr[:, 10:11], rtr[:, 9:10]))
                    vch(e, s_v, e.tensor_scalar_mul(rtr[:, 10:11], rtr[:, 10:11], 0.5))
                    vch(e, s_v, e.tensor_tensor(rtr[:, 14:15], rtr[:, 14:15],
                                                rtr[:, 10:11], ALU.mult))
                    vch(e, s_v, e.tensor_tensor(rtr[:, 15:16], rtr[:, 15:16],
                                                rtr[:, 10:11], ALU.mult))
                    vch(e, s_v, e.tensor_scalar(rtr[:, 16:16 + NE], rtr[:, 32:32 + NE],
                                                rtr[:, 14:15], 0.0, ALU.mult, ALU.add))
                    vch(e, s_v, e.tensor_scalar(rtr[:, 24:24 + NE], rtr[:, 40:40 + NE],
                                                rtr[:, 15:16], 0.0, ALU.mult, ALU.add))
                    ins = e.tensor_tensor(gts[:, blk * NE:(blk + 1) * NE],
                                          rtr[:, 16:16 + NE], rtr[:, 24:24 + NE], ALU.add)
                    s_v.inc(ins, 1)
                rtr_last[0] = s_v.n
        router_done = s_v.n
        xn_ready = s_sc.n

        glob_y_last = [0]
        w2v = w2_bf[:].rearrange("p (k d) -> p k d", d=D)
        for ei in range(NE):
            for c in range(NCH):
                for ft in range(NF):
                    gi = (ei * NCH + c) * NF + ft
                    bank = ps_h[gi % 2]

                    @block.tensor
                    def _(e, ft=ft, gi=gi, c=c, bank=bank):
                        if gi == 0:
                            e.wait_ge(s_sc.h, xn_ready)
                        e.wait_ge(s_w1[gi % 2].h, w1_sent[gi])
                        if gi >= 2:
                            e.wait_ge(s_sc.h, h_act_acks[gi - 2])
                        for dt in range(8):
                            ins = e.matmul(bank[:, :], w1v[:, gi % 2, dt, :],
                                           xnf[:, dt, c * 512:(c + 1) * 512],
                                           start=(dt == 0), stop=(dt == 7))
                        h_group_acks.append(s_t.inc(ins, 1))

                    @block.gpsimd
                    def _(e, gi=gi):
                        nxt = gi + 2
                        if nxt < NTILE_TOT:
                            nei, nfft = w1src_of(nxt)
                            e.wait_ge(s_t.h, h_group_acks[gi])
                            w1_sent.append(
                                s_w1[nxt % 2].inc(e.dma_start(out=w1v[:, nxt % 2, :, :],
                                                              in_=w1_src[nei, :, :, nfft * 128:(nfft + 1) * 128]), 16))

                    @block.scalar
                    def _(e, ei=ei, ft=ft, gi=gi, c=c, bank=bank, gyl=glob_y_last[0]):
                        e.wait_ge(s_t.h, h_group_acks[gi])
                        if gi == 0:
                            e.wait_ge(s_x.h, s_x.n)
                        if ft == 0 and gyl:
                            e.wait_ge(s_t.h, gyl)
                        if ft == 0 and ei > 0:
                            e.wait_ge(s_w.h, 16 + 32 * ei)
                        ins = e.activation(h_bf[:, ft * 512:(ft + 1) * 512],
                                           bank[:, :], AF.Gelu,
                                           bias=b1_sb[:, (ei % 2) * NF + ft:(ei % 2) * NF + ft + 1])
                        h_act_acks.append(s_sc.inc(ins, 1))
                h_done = s_sc.n

                for blk in range(4):
                    for dc in range(2):
                        gj = ((ei * NCH + c) * 4 + blk) * 2 + dc
                        bank = ps_y[gj % 2]

                        @block.tensor
                        def _(e, ei=ei, blk=blk, dc=dc, gj=gj, c=c, bank=bank, hd=h_done):
                            e.wait_ge(s_sc.h, hd)
                            e.wait_ge(s_w2.h, 64 * (ei + 1))
                            e.wait_ge(s_w.h, 16 + 32 * ei)
                            if gj == 0:
                                e.wait_ge(s_v.h, router_done)
                            if gj >= 2:
                                e.wait_ge(s_v.h, y_evac_acks[gj - 2])
                            h3 = h_bf[:].rearrange("p (k t) -> p k t", t=512)
                            for ft in range(NF):
                                e.matmul(bank[:, :], h3[:, ft, blk * 128:(blk + 1) * 128],
                                         w2v[:, ft, dc * 512:(dc + 1) * 512],
                                         start=(ft == 0), stop=False)
                            ins = e.matmul(bank[:, :], ones1[:, :],
                                           b2_sb[0:1, dc * 512:(dc + 1) * 512],
                                           start=False, stop=True)
                            glob_y_last[0] = s_t.inc(ins, 1)

                        @block.vector
                        def _(e, ei=ei, blk=blk, dc=dc, bank=bank, c=c):
                            e.wait_ge(s_t.h, s_t.n)
                            col = (4 * c + blk) * NE + ei
                            dst = x3[:, 4 * c + blk, dc * 512:(dc + 1) * 512]
                            ins = e.scalar_tensor_tensor(dst, bank[:, :],
                                                         gts[:, col:col + 1],
                                                         dst, ALU.mult, ALU.add)
                            y_evac_acks.append(s_v.inc(ins, 1))

            @block.gpsimd
            def _(e, ei=ei, gyl=glob_y_last[0]):
                if ei + 1 < NE:
                    e.wait_ge(s_t.h, gyl)
                    q = NF // 4
                    nb2 = (ei + 1) % 2
                    s_w.inc(e.dma_start(out=b2_sb[0:1, :],
                                        in_=b2_in[ei + 1:ei + 2, :]), 16)
                    s_w.inc(e.dma_start(out=b1_sb[:, nb2 * NF:(nb2 + 1) * NF],
                                        in_=b1_in[:, (ei + 1) * NF:(ei + 2) * NF]), 16)
                    for i in range(4):
                        s_w2.inc(e.dma_start(out=w2v[:, q * i:q * (i + 1), :],
                                             in_=w2_src[ei + 1, :, q * i:q * (i + 1), :]), 16)

        @block.sync
        def _(e):
            e.wait_ge(s_v.h, s_v.n)
            for c in range(NCH):
                s_st.inc(e.dma_start(out=y_dst[:, 4 * c:4 * (c + 1), :],
                                     in_=x3[:, 4 * c:4 * (c + 1), :]), 16)
            e.wait_ge(s_st.h, s_st.n)
    return nc


def run_moe_launch(x_full, W_router, W1, b1, W2, b2):
    T = x_full.shape[0] // NCORE
    NE, _, F = W1.shape
    NF = F // 128
    nc = build_moe(T=T, F=F, NE=NE)
    b1c = np.ascontiguousarray(
        b1.reshape(NE, NF, 128).transpose(2, 0, 1).reshape(128, NE * NF))
    b2r = np.ascontiguousarray(b2)
    eye = _eye_bf()
    in_maps = [{"x": np.ascontiguousarray(x_full[c * T:(c + 1) * T]),
                "wr": W_router, "w1": W1, "b1c": b1c, "w2": W2, "b2r": b2r,
                "eye": eye} for c in range(NCORE)]
    res = run_bass_kernel_spmd(nc, in_maps, core_ids=list(range(NCORE)))
    return np.concatenate([res.results[c]["y"] for c in range(NCORE)], axis=0)




def build_l3attn(S=4096, NHEAD=4, dbg_head=None):
    nc = bass.Bass("TRN2", target_bir_lowering=False)
    NCH = S // 512
    NBLK = S // 128
    HV = NHEAD * 128
    scale = 128.0 ** -0.5
    x_in = nc.dram_tensor("x", [S, D], F32, kind="ExternalInput").ap()
    wq_in = nc.dram_tensor("wq", [D, HV], F32, kind="ExternalInput").ap()
    wk_in = nc.dram_tensor("wk", [D, HV], F32, kind="ExternalInput").ap()
    wv_in = nc.dram_tensor("wv", [D, HV], F32, kind="ExternalInput").ap()
    ww_in = nc.dram_tensor("ww", [D, HV], F32, kind="ExternalInput").ap()
    wbg_in = nc.dram_tensor("wbg", [D, 2 * NHEAD], F32, kind="ExternalInput").ap()
    convw_in = nc.dram_tensor("convw", [128, NHEAD * 4], F32, kind="ExternalInput").ap()
    qnw_in = nc.dram_tensor("qnw", [128, 1], F32, kind="ExternalInput").ap()
    knw_in = nc.dram_tensor("knw", [128, 1], F32, kind="ExternalInput").ap()
    eye_in = nc.dram_tensor("eye", [128, 128], BF16, kind="ExternalInput").ap()
    eyef_in = nc.dram_tensor("eyef", [128, 128], F32, kind="ExternalInput").ap()
    sln_in = nc.dram_tensor("strilneg", [128, 128], F32, kind="ExternalInput").ap()
    tun_in = nc.dram_tensor("triuneg", [128, 128], F32, kind="ExternalInput").ap()
    sun_in = nc.dram_tensor("striuneg", [128, 128], F32, kind="ExternalInput").ap()
    mq_in = nc.dram_tensor("maskq", [4, 128, 512], F32, kind="ExternalInput").ap()
    dmk_in = nc.dram_tensor("diagmask", [128, 128], F32, kind="ExternalInput").ap()
    t1t_in = nc.dram_tensor("tril1t", [128, 128], BF16, kind="ExternalInput").ap()
    o_out = nc.dram_tensor("o", [HV, S], F32, kind="ExternalOutput").ap()
    if dbg_head is not None:
        dbg_out = nc.dram_tensor("dbg", [128, 2 * S], F32,
                                 kind="ExternalOutput").ap()
    x_src = x_in.rearrange("(t p) d -> p t d", p=128)

    with ExitStack() as ctx:
        sb = lambda name, shape, dt: ctx.enter_context(nc.sbuf_tensor(name, shape, dt))
        ps = lambda name, shape, dt: ctx.enter_context(nc.psum_tensor(name, shape, dt))
        sem = lambda name: Sem(ctx.enter_context(nc.semaphore(name)))
        SB4 = max(S, 4 * D)
        xn_fm = sb("xn_fm", [128, 8 * S], BF16)
        qs_fm = sb("qs_fm", [128, S], BF16)
        k_fm = sb("k_fm", [128, S], BF16)
        w_fm = sb("w_fm", [128, S], BF16)
        wtok = sb("wtok", [128, S], BF16)
        va = sb("va", [128, S], BF16)
        sneg_f = sb("sneg_f", [128, SB4], F32)
        sneg_b = sb("sneg_b", [128, S], BF16)
        xtil = sb("xtil", [128, SB4], BF16)
        pre1 = sb("pre1", [128, 512], F32)
        gbc = sb("gbc", [128, S], F32)
        obuf = sb("obuf", [128, 128], BF16)
        u1 = sb("u1", [128, 128], F32)
        l1v = sb("l1v", [128, 128], F32)
        vt1 = sb("vt1", [128, 128], F32)
        vtt1 = sb("vtt1", [128, 128], F32)
        wprev = sb("wprev", [128, S], BF16)
        betag = sb("betag", [128, NBLK * 8], F32)
        gneg_tok = sb("gneg_tok", [128, NBLK], F32)
        totrow = sb("totrow", [1, NBLK], F32)
        offrow = sb("offrow", [1, NBLK], F32)
        mrow = sb("mrow", [1, 128], F32)
        mbc = sb("mbc", [128, 128], F32)
        tqm = sb("tqm", [128, 128], BF16)
        tmpf = sb("tmpf", [128, 512], F32)
        tmpg = sb("tmpg", [128, 512], F32)
        tmpr = sb("tmpr", [1, 512], F32)
        p_bf = sb("p_bf", [128, 128], BF16)
        onum = sb("onum", [128, 128], BF16)
        mtmp = sb("mtmp", [128, 12], F32)
        wq_sb = sb("wq_sb", [128, 8 * 128], BF16)
        wk_sb = sb("wk_sb", [128, 8 * 128], BF16)
        wv_sb = sb("wv_sb", [128, 8 * 128], BF16)
        ww_sb = sb("ww_sb", [128, 8 * 128], BF16)
        wbg_sb = sb("wbg_sb", [128, 8 * 2 * NHEAD], BF16)
        convw_sb = sb("convw_sb", [128, NHEAD * 4], F32)
        qnw_sb = sb("qnw_sb", [128, 1], F32)
        knw_sb = sb("knw_sb", [128, 1], F32)
        qns_sb = sb("qns_sb", [128, 1], F32)
        eye_sb = sb("eye_sb", [128, 128], BF16)
        eyef_sb = sb("eyef_sb", [128, 128], F32)
        sln_sb = sb("sln_sb", [128, 128], F32)
        tun_sb = sb("tun_sb", [128, 128], F32)
        sun_sb = sb("sun_sb", [128, 128], F32)
        mq_sb = sb("mq_sb", [128, 4 * 512], F32)
        dmk_sb = sb("dmk_sb", [128, 128], F32)
        t1t_sb = sb("t1t_sb", [128, 128], BF16)
        ones1f = sb("ones1f", [1, 128], F32)
        onesc = sb("onesc", [128, 1], F32)
        onescb = sb("onescb", [128, 1], BF16)
        x_st = sneg_f
        xn_bf = xtil
        ps0 = ps("ps0", [128, 512], F32)
        ps1 = ps("ps1", [128, 512], F32)
        ps2 = ps("ps2", [128, 512], F32)
        ps_o = ps("ps_o", [128, 128], F32)
        ps_d = ps("ps_d", [128, 512], F32)
        ps_t0 = ps("ps_t0", [128, 512], BF16)
        ps_t1 = ps("ps_t1", [128, 512], BF16)
        stats = ps2[:, 0:64]
        s_x, s_w, s_v = sem("s_x"), sem("s_w"), sem("s_v")
        s_sc, s_t, s_st = sem("s_sc"), sem("s_t"), sem("s_st")
        s_g = sem("s_g")
        block = ctx.enter_context(nc.Block())
        ps_t = [ps_t0, ps_t1]
        x3 = x_st[:].rearrange("p (t d) -> p t d", d=D)
        mq3l = mq_sb[:].rearrange("p (a c) -> p a c", a=4)
        xnf = xn_fm[:].rearrange("p (k t) -> p k t", t=S)
        wbg3 = wbg_sb[:].rearrange("p (k c) -> p k c", k=8)

        def ALLB(e):
            e.wait_ge(s_t.h, s_t.n)
            e.wait_ge(s_v.h, s_v.n)
            e.wait_ge(s_sc.h, s_sc.n)
            e.wait_ge(s_g.h, s_g.n)
            e.wait_ge(s_st.h, s_st.n)
        TB = VB = SB = GB = ALLB

        def t_ops(fn):
            @block.tensor
            def _(e):
                TB(e)
                ins = fn(e)
                s_t.inc(ins, 1)

        def v_ops(fn):
            @block.vector
            def _(e):
                VB(e)
                ins = fn(e)
                s_v.inc(ins, 1)

        def sc_ops(fn):
            @block.scalar
            def _(e):
                SB(e)
                ins = fn(e)
                s_sc.inc(ins, 1)

        def g_ops(fn):
            @block.gpsimd
            def _(e):
                GB(e)
                ins = fn(e)
                s_g.inc(ins, 16)

        @block.sync
        def _(e):
            s_x.inc(e.dma_start(out=x3[:, :, :], in_=x_src[:, 0:4, :]), 16)
            for t, a in [(eye_sb, eye_in), (eyef_sb, eyef_in), (sln_sb, sln_in),
                         (tun_sb, tun_in), (sun_sb, sun_in), (dmk_sb, dmk_in),
                         (t1t_sb, t1t_in), (convw_sb, convw_in), (qnw_sb, qnw_in),
                         (knw_sb, knw_in)]:
                s_x.inc(e.dma_start(out=t[:, :], in_=a[:, :]), 16)
            for a in range(4):
                s_x.inc(e.dma_start(out=mq3l[:, a, :], in_=mq_in[a, :, :]), 16)
        PRO_X = s_x.n

        @block.gpsimd
        def _(e):
            s_w.inc(e.dma_start(out=wbg3,
                                in_=wbg_in.rearrange("(k p) c -> p k c", p=128)), 16)
        PRO_W = s_w.n

        @block.vector
        def _(e):
            e.wait_ge(s_x.h, PRO_X)
            e.memset(ones1f[:, :], 1.0)
            e.memset(onesc[:, :], 1.0)
            e.memset(onescb[:, :], 1.0)
            e.memset(sneg_b[:, :], 0.0)
            s_v.inc(e.tensor_scalar_mul(qns_sb[:, :], qnw_sb[:, :], scale), 1)

        for c in range(NCH):
            def pre(e, c=c):
                e.wait_ge(s_x.h, 16 * (c + 1) + (PRO_X - 16))
                if c > 0:
                    e.wait_ge(s_t.h, s_t.n)
            emit_ln_chunk(block, nc,
                          lambda b, c=c: x3[:, b, :],
                          lambda b: xn_bf[:, b * D:(b + 1) * D],
                          stats, s_v, s_sc, nblk=4, pre_wait=pre)
            la = s_v.n

            @block.sync
            def _(e, c=c, la=la):
                if c + 1 < NCH:
                    e.wait_ge(s_v.h, la)
                    s_x.inc(e.dma_start(out=x3[:, :, :],
                                        in_=x_src[:, 4 * (c + 1):4 * (c + 2), :]), 16)

            @block.tensor
            def _(e, la=la):
                e.wait_ge(s_v.h, la)
                e.wait_ge(s_x.h, PRO_X)
                e.wait_ge(s_w.h, PRO_W)
            emit_transpose_chunk(
                block, nc,
                lambda g, j, c=c: xn_bf[:, j * D + g * 128: j * D + g * 128 + 128],
                lambda g, c=c: xnf[:, g, c * 512:(c + 1) * 512],
                eye_sb, ps_t, s_t, s_sc, ngroups=8)
        xn_ready = s_sc.n

        @block.tensor
        def _(e):
            e.wait_ge(s_sc.h, xn_ready)

        def rms_rows_norm(c, pa, dst, wcol, mean_div, eps):
            sc_ops(lambda e: e.activation(tmpf[:, :], pa[:, :], AF.Square))
            t_ops(lambda e: e.matmul(ps_d[0:1, 0:256], onesc[:, :], tmpf[:, 0:256],
                                     start=True, stop=True))
            t_ops(lambda e: e.matmul(ps_d[0:1, 256:512], onesc[:, :], tmpf[:, 256:512],
                                     start=True, stop=True))
            v_ops(lambda e: e.tensor_scalar(tmpr[0:1, 0:512], ps_d[0:1, 0:512],
                                            1.0 / mean_div, eps, ALU.mult, ALU.add))
            sc_ops(lambda e: e.sqrt(tmpr[0:1, :], tmpr[0:1, :]))
            v_ops(lambda e: e.reciprocal(tmpr[0:1, :], tmpr[0:1, :]))
            t_ops(lambda e: e.matmul(ps2[:, :], ones1f[0:1, :], tmpr[0:1, :],
                                     start=True, stop=True))
            sc_ops(lambda e: e.copy(tmpf[:, :], pa[:, :]))
            v_ops(lambda e: e.tensor_tensor(tmpg[:, :], tmpf[:, :], ps2[:, :], ALU.mult))
            if wcol is not None:
                v_ops(lambda e: e.tensor_scalar(dst[:, c * 512:(c + 1) * 512],
                                                tmpg[:, :], wcol[:, 0:1], 0.0,
                                                ALU.mult, ALU.add))
            else:
                v_ops(lambda e: e.tensor_copy(dst[:, c * 512:(c + 1) * 512], tmpg[:, :]))

        for h in range(NHEAD):
            for wt, wi in ((wq_sb, wq_in), (wk_sb, wk_in), (wv_sb, wv_in),
                           (ww_sb, ww_in)):
                g_ops(lambda e, wt=wt, wi=wi, h=h: e.dma_start(
                    out=wt[:].rearrange("p (k c) -> p k c", k=8),
                    in_=wi.rearrange("(k p) c -> p k c", p=128)[:, :, h * 128:(h + 1) * 128]))
            for which, w3t in (("q", wq_sb), ("k", wk_sb), ("v", wv_sb)):
                w3 = w3t[:].rearrange("p (k c) -> p k c", k=8)
                for c in range(NCH):
                    pa = ps0

                    def mmproj(e, c=c, w3=w3, h=h, pa=pa):
                        for dt in range(8):
                            ins = e.matmul(pa[:, :], w3[:, dt, :],
                                           xnf[:, dt, c * 512:(c + 1) * 512],
                                           start=(dt == 0), stop=(dt == 7))
                        return ins
                    t_ops(mmproj)
                    if which == "q":
                        rms_rows_norm(c, pa, qs_fm, qns_sb, 128.0, 1e-6)
                    elif which == "k":
                        rms_rows_norm(c, pa, k_fm, knw_sb, 128.0, 1e-6)
                    else:
                        sc_ops(lambda e, c=c, pa=pa: e.copy(
                            wprev[:, c * 512:(c + 1) * 512], pa[:, :]))
            if dbg_head == h:
                @block.gpsimd
                def _(e):
                    GB(e)
                    s_st.inc(e.dma_start(out=dbg_out[:, 0:S], in_=qs_fm[:, :]), 16)
                    s_st.inc(e.dma_start(out=dbg_out[:, S:2 * S], in_=k_fm[:, :]),
                             16)
            for i in range(NBLK):
                bank = ps_t[i % 2]
                sc_ops(lambda e, i=i: e.copy(p_bf[:, :], wprev[:, i * 128:(i + 1) * 128]))
                t_ops(lambda e, i=i, bank=bank: e.transpose(bank[:, 0:128], p_bf[:, :],
                                                            eye_sb[:, :]))
                sc_ops(lambda e, i=i, bank=bank: e.copy(va[:, i * 128:(i + 1) * 128],
                                                        bank[0:128, 0:128]))
            for c in range(NCH):
                pa = ps0

                def mmw(e, c=c, h=h, pa=pa):
                    w3 = ww_sb[:].rearrange("p (k c) -> p k c", k=8)
                    for dt in range(8):
                        ins = e.matmul(pa[:, :], w3[:, dt, :],
                                       xnf[:, dt, c * 512:(c + 1) * 512],
                                       start=(dt == 0), stop=(dt == 7))
                    return ins
                t_ops(mmw)
                sc_ops(lambda e, c=c, pa=pa: e.copy(wprev[:, c * 512:(c + 1) * 512],
                                                    pa[:, :]))
            def conv_ops(e, h=h):
                cw = convw_sb
                e.tensor_scalar(gbc[:, :], wprev[:, :], cw[:, h * 4 + 3:h * 4 + 4],
                                0.0, ALU.mult, ALU.add).then_inc(s_v.h, 1)
                s_v.n += 1
                e.wait_ge(s_v.h, s_v.n)
                for dlt in (1, 2, 3):
                    ins = e.scalar_tensor_tensor(
                        gbc[:, dlt:S], wprev[:, 0:S - dlt],
                        cw[:, h * 4 + 3 - dlt:h * 4 + 4 - dlt],
                        gbc[:, dlt:S], ALU.mult, ALU.add)
                    ins.then_inc(s_v.h, 1)
                    s_v.n += 1
                    e.wait_ge(s_v.h, s_v.n)
                return ins

            @block.vector
            def _(e):
                VB(e)
                conv_ops(e)
            sc_ops(lambda e: e.activation(wprev[:, :], gbc[:, :], AF.Silu))
            for c in range(NCH):
                pa_view = wprev[:, c * 512:(c + 1) * 512]
                sc_ops(lambda e, pa_view=pa_view: e.activation(tmpf[:, :], pa_view,
                                                               AF.Square))
                t_ops(lambda e: e.matmul(ps_d[0:1, 0:256], onesc[:, :], tmpf[:, 0:256],
                                         start=True, stop=True))
                t_ops(lambda e: e.matmul(ps_d[0:1, 256:512], onesc[:, :], tmpf[:, 256:512],
                                         start=True, stop=True))
                v_ops(lambda e: e.tensor_scalar(tmpr[0:1, 0:512], ps_d[0:1, 0:512],
                                                1.0, 1e-6, ALU.mult, ALU.add))
                sc_ops(lambda e: e.sqrt(tmpr[0:1, :], tmpr[0:1, :]))
                v_ops(lambda e: e.reciprocal(tmpr[0:1, :], tmpr[0:1, :]))
                t_ops(lambda e: e.matmul(ps2[:, :], ones1f[0:1, :], tmpr[0:1, :],
                                         start=True, stop=True))
                v_ops(lambda e, c=c, pa_view=pa_view: e.tensor_tensor(
                    w_fm[:, c * 512:(c + 1) * 512], pa_view, ps2[:, :], ALU.mult))
            for i in range(NBLK):
                bank = ps_t[i % 2]
                t_ops(lambda e, i=i, bank=bank: e.transpose(
                    bank[:, 0:128], w_fm[:, i * 128:(i + 1) * 128], eye_sb[:, :]))
                sc_ops(lambda e, i=i, bank=bank: e.copy(
                    wtok[:, i * 128:(i + 1) * 128], bank[0:128, 0:128]))

            for c in range(NCH):
                def mmbg(e, c=c, h=h):
                    for dt in range(8):
                        ins = e.matmul(ps_d[0:2 * NHEAD, 0:512], wbg3[:, dt, :],
                                       xnf[:, dt, c * 512:(c + 1) * 512],
                                       start=(dt == 0), stop=(dt == 7))
                    return ins
                t_ops(mmbg)
                for j in range(4):
                    bank = ps_t[j % 2]
                    sc_ops(lambda e, j=j: e.copy(p_bf[0:2 * NHEAD, :],
                                                 ps_d[0:2 * NHEAD, j * 128:(j + 1) * 128]))
                    t_ops(lambda e, j=j, bank=bank: e.transpose(
                        bank[:, 0:2 * NHEAD], p_bf[0:2 * NHEAD, 0:128],
                        eye_sb[0:2 * NHEAD, 0:2 * NHEAD]))
                    v_ops(lambda e, c=c, j=j, bank=bank: e.tensor_copy(
                        betag[:].rearrange("p (b j) -> p b j", j=8)[:, 4 * c + j, :],
                        bank[0:128, 0:2 * NHEAD]))
            bg4 = betag[:].rearrange("p (b j) -> p b j", j=8)
            for blk in range(NBLK):
                sc_ops(lambda e, blk=blk: e.activation(
                    bg4[:, blk, h:h + 1], bg4[:, blk, h:h + 1], AF.Sigmoid))
                v_ops(lambda e, blk=blk: e.tensor_scalar_mul(
                    bg4[:, blk, h:h + 1], bg4[:, blk, h:h + 1], 2.0))
                u = lambda blk=blk: bg4[:, blk, NHEAD + h:NHEAD + h + 1]
                v_ops(lambda e, blk=blk: e.tensor_scalar_min(mtmp[:, 3:4], u(blk), 0.0))
                v_ops(lambda e, blk=blk: e.tensor_scalar_max(mtmp[:, 0:1], u(blk), 0.0))
                v_ops(lambda e: e.tensor_tensor(mtmp[:, 0:1], mtmp[:, 0:1],
                                                mtmp[:, 3:4], ALU.subtract))
                sc_ops(lambda e: e.activation(mtmp[:, 1:2], mtmp[:, 0:1], AF.Exp,
                                              scale=-1.0))
                sc_ops(lambda e: e.activation(mtmp[:, 2:3], mtmp[:, 1:2], AF.Ln,
                                              bias=1.0))
                v_ops(lambda e, blk=blk: e.tensor_tensor(mtmp[:, 4:5], mtmp[:, 3:4],
                                                         mtmp[:, 2:3], ALU.subtract))
                sc_ops(lambda e: e.copy(p_bf[:, 0:1], mtmp[:, 4:5]))
                t_ops(lambda e: e.matmul(ps_d[:, 0:1], t1t_sb[:, :], p_bf[:, 0:1],
                                         start=True, stop=True))
                v_ops(lambda e, blk=blk: e.tensor_scalar_mul(
                    gneg_tok[:, blk:blk + 1], ps_d[:, 0:1], -1.0))
            g_ops(lambda e: e.dma_start(out=totrow[0:1, :],
                                        in_=gneg_tok[127:128, :]))
            v_ops(lambda e: e.memset(offrow[0:1, :], 0.0))
            for sh in (1, 2, 4, 8, 16):
                if sh < NBLK:
                    v_ops(lambda e, sh=sh: e.tensor_tensor(
                        offrow[0:1, sh:NBLK], offrow[0:1, sh:NBLK]
                        if False else totrow[0:1, 0:NBLK - sh],
                        offrow[0:1, sh:NBLK], ALU.add))
            v_ops(lambda e: e.tensor_copy(offrow[0:1, :], totrow[0:1, :]))
            for sh in (1, 2, 4, 8, 16):
                if sh < NBLK:
                    v_ops(lambda e, sh=sh: e.tensor_tensor(
                        offrow[0:1, sh:NBLK], offrow[0:1, 0:NBLK - sh],
                        offrow[0:1, sh:NBLK], ALU.add))
            v_ops(lambda e: e.memset(totrow[0:1, 0:1], 0.0))
            if NBLK > 1:
                v_ops(lambda e: e.tensor_copy(totrow[0:1, 1:NBLK],
                                              offrow[0:1, 0:NBLK - 1]))
            t_ops(lambda e: e.matmul(ps_d[:, 0:NBLK], ones1f[0:1, :],
                                     totrow[0:1, 0:NBLK], start=True, stop=True))
            v_ops(lambda e: e.tensor_tensor(gneg_tok[:, :], gneg_tok[:, :],
                                            ps_d[:, 0:NBLK], ALU.add))
            sc_ops(lambda e: e.copy(tmpf[:, 0:NBLK], gneg_tok[:, :]))
            t_ops(lambda e: e.transpose(ps_d[0:NBLK, 0:128], tmpf[:, 0:NBLK],
                                        eyef_sb[:, :]))
            sc_ops(lambda e: e.copy(mbc[0:NBLK, 0:128], ps_d[0:NBLK, 0:128]))
            for c in range(NCH):
                @block.gpsimd
                def _(e, c=c):
                    GB(e)
                    for j in range(4):
                        s_g.inc(e.dma_start(
                            out=tmpr[0:1, j * 128:(j + 1) * 128],
                            in_=mbc[4 * c + j:4 * c + j + 1, 0:128]), 16)
                t_ops(lambda e, c=c: e.matmul(ps2[:, :], ones1f[0:1, :],
                                              tmpr[0:1, 0:512],
                                              start=True, stop=True))
                v_ops(lambda e, c=c: e.tensor_copy(gbc[:, c * 512:(c + 1) * 512],
                                                   ps2[:, :]))

            v_ops(lambda e: e.memset(sneg_f[:, :], 0.0))
            v_ops(lambda e: e.memset(sneg_b[:, :], 0.0))
            for I in range(NBLK):
                ncc = I // 4 + 1
                wI = lambda: w_fm[:, I * 128:(I + 1) * 128]
                qI = lambda: qs_fm[:, I * 128:(I + 1) * 128]
                t_ops(lambda e, I=I: e.matmul(ps_o[:, :], wI(), wI(),
                                              start=True, stop=True))
                v_ops(lambda e: e.tensor_tensor(tmpf[:, 0:128], ps_o[:, :],
                                                sun_sb[:, :], ALU.mult))
                v_ops(lambda e, I=I: e.tensor_scalar(
                    u1[:, :], tmpf[:, 0:128], bg4[:, I, h:h + 1], 0.0,
                    ALU.mult, ALU.add))
                t_ops(lambda e: e.transpose(ps_d[:, 0:128], u1[:, :], eyef_sb[:, :]))
                sc_ops(lambda e: e.copy(l1v[:, :], ps_d[:, 0:128]))
                v_ops(lambda e: e.tensor_tensor(vt1[:, :], u1[:, :], eyef_sb[:, :],
                                                ALU.add))
                v_ops(lambda e: e.tensor_tensor(vtt1[:, :], l1v[:, :], eyef_sb[:, :],
                                                ALU.add))
                for kk in range(1, 7):
                    t_ops(lambda e: e.matmul(ps_o[:, :], l1v[:, :], u1[:, :],
                                             start=True, stop=True))
                    t_ops(lambda e: e.matmul(ps_d[:, 0:128], u1[:, :], l1v[:, :],
                                             start=True, stop=True))
                    sc_ops(lambda e: e.copy(tmpf[:, 0:128], ps_o[:, :]))
                    sc_ops(lambda e: e.copy(tmpg[:, 0:128], ps_d[:, 0:128]))
                    v_ops(lambda e: e.tensor_copy(u1[:, :], tmpf[:, 0:128]))
                    v_ops(lambda e: e.tensor_copy(l1v[:, :], tmpg[:, 0:128]))
                    t_ops(lambda e: e.matmul(ps_o[:, :], vtt1[:, :], u1[:, :],
                                             start=True, stop=True))
                    t_ops(lambda e: e.matmul(ps_d[:, 0:128], u1[:, :], vtt1[:, :],
                                             start=True, stop=True))
                    sc_ops(lambda e: e.copy(tmpf[:, 0:128], ps_o[:, :]))
                    sc_ops(lambda e: e.copy(tmpg[:, 0:128], ps_d[:, 0:128]))
                    v_ops(lambda e: e.tensor_tensor(vt1[:, :], vt1[:, :],
                                                    tmpf[:, 0:128], ALU.add))
                    v_ops(lambda e: e.tensor_tensor(vtt1[:, :], vtt1[:, :],
                                                    tmpg[:, 0:128], ALU.add))
                for cc in range(ncc):
                    def mmpre(e, cc=cc, I=I):
                        e.matmul(ps0[:, :], wI(), k_fm[:, cc * 512:(cc + 1) * 512],
                                 start=True, stop=False)
                        return e.matmul(ps0[:, :], wI(),
                                        sneg_b[:, cc * 512:(cc + 1) * 512],
                                        start=False, stop=True)
                    t_ops(mmpre)
                    if cc == I // 4:
                        sp = I % 4
                        v_ops(lambda e: e.memset(tmpf[:, :], 0.0))
                        if sp > 0:
                            sc_ops(lambda e, sp=sp: e.copy(
                                tmpf[:, 0:sp * 128], ps0[:, 0:sp * 128]))
                        v_ops(lambda e, sp=sp: e.scalar_tensor_tensor(
                            tmpg[:, 0:128], sln_sb[:, :], -1.0,
                            ps0[:, sp * 128:(sp + 1) * 128], ALU.mult, ALU.mult))
                        v_ops(lambda e, sp=sp: e.tensor_copy(
                            tmpf[:, sp * 128:(sp + 1) * 128], tmpg[:, 0:128]))
                        v_ops(lambda e: e.tensor_copy(pre1[:, :], tmpf[:, :]))
                    else:
                        sc_ops(lambda e: e.copy(pre1[:, :], ps0[:, :]))
                    t_ops(lambda e: e.matmul(ps0[:, :], vt1[:, :], pre1[:, :],
                                             start=True, stop=True))
                    v_ops(lambda e, cc=cc, I=I: e.tensor_scalar(
                        xtil[:, cc * 512:(cc + 1) * 512], ps0[:, :],
                        bg4[:, I, h:h + 1], 0.0, ALU.mult, ALU.add))
                t_ops(lambda e, I=I: e.matmul(ps_o[:, :], wI(), qI(),
                                              start=True, stop=True))
                v_ops(lambda e: e.tensor_tensor(tmpf[:, 0:128], ps_o[:, :],
                                                tun_sb[:, :], ALU.mult))
                v_ops(lambda e: e.tensor_copy(tqm[:, :], tmpf[:, 0:128]))
                for cc in range(ncc):
                    def mmp1(e, cc=cc, I=I):
                        e.matmul(ps1[:, :], qI(), k_fm[:, cc * 512:(cc + 1) * 512],
                                 start=True, stop=False)
                        e.matmul(ps1[:, :], qI(), sneg_b[:, cc * 512:(cc + 1) * 512],
                                 start=False, stop=False)
                        return e.matmul(ps1[:, :], tqm[:, :],
                                        xtil[:, cc * 512:(cc + 1) * 512],
                                        start=False, stop=True)
                    t_ops(mmp1)
                    if cc == I // 4:
                        sp = I % 4
                        v_ops(lambda e, cc=cc: e.tensor_tensor(
                            tmpf[:, :], ps1[:, :], gbc[:, cc * 512:(cc + 1) * 512],
                            ALU.add))
                        v_ops(lambda e, sp=sp: e.tensor_tensor(
                            tmpf[:, :], tmpf[:, :], mq3l[:, sp, :], ALU.add))
                        v_ops(lambda e, cc=cc: e.reduce_max(
                            mtmp[:, 5 + cc % 4:6 + cc % 4], tmpf[:, :],
                            axis=AX.X, negate=True))
                    else:
                        v_ops(lambda e, cc=cc: e.tensor_tensor(
                            tmpf[:, :], ps1[:, :], gbc[:, cc * 512:(cc + 1) * 512],
                            ALU.add))
                        v_ops(lambda e, cc=cc: e.reduce_max(
                            mtmp[:, 5 + cc % 4:6 + cc % 4], tmpf[:, :],
                            axis=AX.X, negate=True))
                    if cc % 4 == 3 or cc == ncc - 1:
                        nred = cc % 4 + 1
                        v_ops(lambda e, nred=nred: e.tensor_reduce(
                            mtmp[:, 9:10] if cc >= 4 else mtmp[:, 9:10],
                            mtmp[:, 5:5 + nred], axis=AX.X, op=ALU.min))
                        if cc >= 4:
                            v_ops(lambda e: e.tensor_tensor(
                                mtmp[:, 10:11], mtmp[:, 10:11], mtmp[:, 9:10],
                                ALU.min))
                        else:
                            v_ops(lambda e: e.tensor_copy(mtmp[:, 10:11],
                                                          mtmp[:, 9:10]))
                sc_ops(lambda e: e.copy(tmpf[:, 0:1], mtmp[:, 10:11]))
                t_ops(lambda e: e.transpose(ps_d[0:1, 0:128], tmpf[:, 0:1],
                                            eyef_sb[:, :]))
                sc_ops(lambda e: e.copy(mrow[0:1, :], ps_d[0:1, 0:128]))
                t_ops(lambda e: e.matmul(ps_d[:, 0:128], ones1f[0:1, :],
                                         mrow[0:1, :], start=True, stop=True))
                v_ops(lambda e: e.tensor_copy(mbc[:, :], ps_d[:, 0:128]))
                for j in range(I + 1):
                    def mmp2(e, j=j, I=I):
                        e.matmul(ps1[:, 0:128], k_fm[:, j * 128:(j + 1) * 128],
                                 qI(), start=True, stop=False)
                        e.matmul(ps1[:, 0:128], sneg_b[:, j * 128:(j + 1) * 128],
                                 qI(), start=False, stop=False)
                        return e.matmul(ps1[:, 0:128], xtil[:, j * 128:(j + 1) * 128],
                                        tqm[:, :], start=False, stop=True)
                    t_ops(mmp2)
                    def p2v(e, j=j, I=I):
                        e.tensor_scalar(tmpf[:, 0:128], ps1[:, 0:128],
                                        gneg_tok[:, j:j + 1], 0.0, ALU.add,
                                        ALU.add).then_inc(s_v.h, 1)
                        s_v.n += 1
                        e.wait_ge(s_v.h, s_v.n)
                        ins = e.tensor_tensor(tmpf[:, 0:128], tmpf[:, 0:128],
                                              mbc[:, :], ALU.add)
                        if j == I:
                            ins.then_inc(s_v.h, 1)
                            s_v.n += 1
                            e.wait_ge(s_v.h, s_v.n)
                            ins = e.tensor_tensor(tmpf[:, 0:128], tmpf[:, 0:128],
                                                  dmk_sb[:, :], ALU.add)
                        return ins

                    @block.vector
                    def _(e, fn=p2v):
                        VB(e)
                        s_v.inc(fn(e), 1)
                    sc_ops(lambda e: e.activation(p_bf[:, :], tmpf[:, 0:128], AF.Exp))
                    def mmpv(e, j=j, I=I):
                        e.matmul(ps_o[:, :], va[:, j * 128:(j + 1) * 128], p_bf[:, :],
                                 start=(j == 0), stop=(j == I), skip_group_check=True)
                        return e.matmul(ps_d[0:1, 0:128], onescb[:, :], p_bf[:, :],
                                        start=(j == 0), stop=(j == I),
                                        skip_group_check=True)
                    t_ops(mmpv)
                v_ops(lambda e: e.reciprocal(tmpr[0:1, 0:128], ps_d[0:1, 0:128]))
                t_ops(lambda e: e.matmul(ps_d[:, 0:128], ones1f[0:1, :],
                                         tmpr[0:1, 0:128], start=True, stop=True))
                sc_ops(lambda e: e.copy(onum[:, :], ps_o[:, :]))
                v_ops(lambda e: e.tensor_tensor(obuf[:, :], onum[:, :],
                                                ps_d[:, 0:128], ALU.mult))
                @block.gpsimd
                def _(e, h=h, I=I):
                    GB(e)
                    s_st.inc(e.dma_start(
                        out=o_out[h * 128:(h + 1) * 128, I * 128:(I + 1) * 128],
                        in_=obuf[:, :]), 16)
                for cc in range(ncc):
                    t_ops(lambda e, cc=cc, I=I: e.matmul(
                        ps0[:, :], wtok[:, I * 128:(I + 1) * 128],
                        xtil[:, cc * 512:(cc + 1) * 512], start=True, stop=True))
                    v_ops(lambda e, cc=cc: e.tensor_tensor(
                        sneg_f[:, cc * 512:(cc + 1) * 512],
                        sneg_f[:, cc * 512:(cc + 1) * 512], ps0[:, :], ALU.subtract))
                    sc_ops(lambda e, cc=cc: e.copy(
                        sneg_b[:, cc * 512:(cc + 1) * 512],
                        sneg_f[:, cc * 512:(cc + 1) * 512]))
        @block.gpsimd
        def _(e):
            e.wait_ge(s_st.h, s_st.n)
    return nc


def l3_consts():
    r = np.arange(128)[:, None]
    c = np.arange(128)[None, :]
    maskq, _ = causal_masks()
    return dict(
        eye=_eye_bf(),
        eyef=np.eye(128, dtype=np.float32),
        strilneg=np.where(r > c, -1.0, 0.0).astype(np.float32),
        triuneg=np.where(r <= c, -1.0, 0.0).astype(np.float32),
        striuneg=np.where(r < c, -1.0, 0.0).astype(np.float32),
        diagmask=np.where(r <= c, 0.0, -1e30).astype(np.float32),
        tril1t=np.triu(np.ones((128, 128))).astype(ml_dtypes.bfloat16),
        maskq=maskq)


def run_l3attn_launch(x_full, Wq, Wk, Wv, Ww, Wbeta, Wg, conv_w, qn_w, kn_w):
    S = x_full.shape[1]
    NH = 4
    nc = build_l3attn(S=S, NHEAD=NH)
    consts = l3_consts()
    qnw = np.ascontiguousarray(qn_w.reshape(-1)[:128].reshape(128, 1)
                               .astype(np.float32))
    knw = np.ascontiguousarray(kn_w.reshape(-1)[:128].reshape(128, 1)
                               .astype(np.float32))
    in_maps = []
    for c in range(NCORE):
        b, hh = c // 2, c % 2
        cols = slice(hh * 512, (hh + 1) * 512)
        heads = range(hh * NH, hh * NH + NH)
        convw = np.ascontiguousarray(np.concatenate(
            [conv_w[h * 128:(h + 1) * 128, :] for h in heads], axis=1)
            .astype(np.float32))
        im = {"x": np.ascontiguousarray(x_full[b]),
              "wq": np.ascontiguousarray(Wq[:, cols]),
              "wk": np.ascontiguousarray(Wk[:, cols]),
              "wv": np.ascontiguousarray(Wv[:, cols]),
              "ww": np.ascontiguousarray(Ww[:, cols]),
              "wbg": np.ascontiguousarray(np.concatenate(
                  [Wbeta[:, hh * NH:hh * NH + NH],
                   Wg[:, hh * NH:hh * NH + NH]], axis=1)),
              "convw": convw, "qnw": qnw, "knw": knw, **consts}
        in_maps.append(im)
    res = run_bass_kernel_spmd(nc, in_maps, core_ids=list(range(NCORE)))
    return [np.concatenate([res.results[2 * b]["o"], res.results[2 * b + 1]["o"]],
                           axis=0) for b in range(4)]


def kernel(**inputs):
    inp = {k: np.asarray(v) for k, v in inputs.items()}
    x = inp["L0_attn_query"].astype(np.float32)
    B, S, Dm = x.shape
    xf = np.ascontiguousarray(x.reshape(B * S, Dm))

    xf = run_mlp_launch(xf, inp["L0_ffn_w1"], inp["L0_ffn_b1"],
                        inp["L0_ffn_w2"], inp["L0_ffn_b2"])
    ctx = run_l0attn_launch(xf.reshape(B, S, Dm), inp["L0_attn_Wq"],
                            inp["L0_attn_Wk"], inp["L0_attn_Wv"])
    xf = run_proj_launch(xf, ctx, inp["L0_attn_Wo"])
    xf = run_mlp_launch(xf, inp["L0_ffn_w1"], inp["L0_ffn_b1"],
                        inp["L0_ffn_w2"], inp["L0_ffn_b2"])

    xf = run_mod_launch(xf, inp["L1_ffn_w_router"], inp["L1_ffn_W1"],
                        inp["L1_ffn_b1"], inp["L1_ffn_W2"], inp["L1_ffn_b2"])
    Wkv = inp["L1_attn_Wkv"]
    ofm = run_causal_launch(xf.reshape(B, S, Dm), inp["L1_attn_Wq"],
                            Wkv[:, 0:Dm], Wkv[:, Dm:2 * Dm], mode="l1",
                            g_ln=inp["L1_attn_g_ln"])
    xf = run_proj_launch(xf, ofm, inp["L1_attn_Wo"])
    xf = run_mod_launch(xf, inp["L1_ffn_w_router"], inp["L1_ffn_W1"],
                        inp["L1_ffn_b1"], inp["L1_ffn_W2"], inp["L1_ffn_b2"])

    xf = run_swiglu_launch(xf, inp["L2_ffn_w_gate"], inp["L2_ffn_w_up"],
                           inp["L2_ffn_w_down"])
    ofm = run_causal_launch(xf.reshape(B, S, Dm), inp["L2_attn_Wq"],
                            inp["L2_attn_Wk"], inp["L2_attn_Wv"], mode="l2")
    xf = run_proj_launch(xf, ofm, inp["L2_attn_Wo"], bo=inp["L2_attn_bo"])
    xf = run_swiglu_launch(xf, inp["L2_ffn_w_gate"], inp["L2_ffn_w_up"],
                           inp["L2_ffn_w_down"])

    xf = run_moe_launch(xf, inp["L3_ffn_W_router"], inp["L3_ffn_W1"],
                        inp["L3_ffn_b1"], inp["L3_ffn_W2"], inp["L3_ffn_b2"])
    ofm = run_l3attn_launch(xf.reshape(B, S, Dm), inp["L3_attn_Wq"],
                            inp["L3_attn_Wk"], inp["L3_attn_Wv"],
                            inp["L3_attn_Ww"], inp["L3_attn_Wbeta"],
                            inp["L3_attn_Wg"], inp["L3_attn_conv_w"],
                            inp["L3_attn_qn_w"], inp["L3_attn_kn_w"])
    xf = run_proj_launch(xf, ofm, inp["L3_attn_Wo"])
    xf = run_moe_launch(xf, inp["L3_ffn_W_router"], inp["L3_ffn_W1"],
                        inp["L3_ffn_b1"], inp["L3_ffn_W2"], inp["L3_ffn_b2"])

    return np.ascontiguousarray(xf.reshape(B, S, Dm).astype(np.float32))
